# version 12
# baseline (speedup 1.0000x reference)
import contextlib
import numpy as np
import ml_dtypes
import concourse.bass as bass
import concourse.mybir as mybir
from concourse.bass_utils import run_bass_kernel_spmd

F32 = mybir.dt.float32
F16 = mybir.dt.float16
BF16 = mybir.dt.bfloat16
I32 = mybir.dt.int32
AF = mybir.ActivationFunctionType
ALU = mybir.AluOpType
AX = mybir.AxisListType

NCORES = 8
S = 2048
D = 1024
NH = 16
DH = 64
DI = 2816
TC = S // 128
QB = S // 512
KC = D // 128
IC = DI // 128
MAG = 1536.0
C2 = 14848.498046875

ENGINES = ("sync", "vector", "scalar", "tensor", "gpsimd")
NSLOT = 6


class Sem:
    def __init__(self, nc, es, name):
        self.h = es.enter_context(nc.semaphore(name))
        self.n = 0
        self.is_dma = False

    def bump(self, k=1):
        self.n += k
        return self.n


class Prog:
    def __init__(self, nc, es):
        self.nc = nc
        self.es = es
        self.q = {e: [] for e in ENGINES}
        self.sems = {}
        self.last_wait = {e: {} for e in ENGINES}

    def sem(self, name):
        if name not in self.sems:
            self.sems[name] = Sem(self.nc, self.es, name)
        return self.sems[name]

    def do(self, eng, fn, deps=(), sem=None, k=1):
        lw = self.last_wait[eng]
        for d in deps:
            if d is None:
                continue
            s, v = d
            if s.is_dma:
                v = s.n
            if v <= 0 or lw.get(s, 0) >= v:
                continue
            lw[s] = v
            self.q[eng].append(("w", s.h, v))
        if sem is not None:
            val = sem.bump(k)
            self.q[eng].append(("o", fn, sem.h, k))
            return (sem, val)
        self.q[eng].append(("o", fn, None, 0))
        return None

    def dma(self, eng, out_ap, in_ap, deps=(), sem=None, transpose=False):
        if sem is None:
            sem = self.sem("dma_" + eng)
        sem.is_dma = True
        deps = list(deps) + [(sem, sem.n)]
        if transpose:
            fn = lambda e, o=out_ap, i=in_ap: e.dma_start(out=o, in_=i, transpose=True)
        else:
            fn = lambda e, o=out_ap, i=in_ap: e.dma_start(out=o, in_=i)
        return self.do(eng, fn, deps=deps, sem=sem, k=16)

    def build(self):
        with self.nc.Block() as block:
            for e in ENGINES:
                items = self.q[e]
                if not items:
                    continue

                def emit(eng, items=items):
                    for it in items:
                        if it[0] == "w":
                            eng.wait_ge(it[1], it[2])
                        else:
                            ins = it[1](eng)
                            if it[2] is not None:
                                ins.then_inc(it[2], it[3])

                getattr(block, e)(emit)


def host_consts():
    inv_freq = 1.0 / (10000.0 ** (np.arange(0, DH, 2, dtype=np.float64) / DH))
    t = np.arange(S, dtype=np.float64)
    ang = np.outer(t, inv_freq)
    rows = (np.arange(128) % DH) // 2
    C128 = np.cos(ang).T[rows, :].astype(np.float32)
    S128 = np.sin(ang).T[rows, :].astype(np.float32)
    P = np.zeros((128, 128), np.float32)
    for blk in range(2):
        o = blk * 64
        for i in range(DH // 2):
            P[o + 2 * i, o + 2 * i + 1] = -1.0
            P[o + 2 * i + 1, o + 2 * i] = 1.0
    P_T = np.ascontiguousarray(P.T).astype(ml_dtypes.bfloat16)
    tri = (np.arange(128)[:, None] <= np.arange(128)[None, :]).astype(
        ml_dtypes.bfloat16)
    ones_row = np.ones((1, 128), np.float32)
    ident = np.eye(128, dtype=np.float32)
    return C128, S128, P_T, tri, ones_row, ident


def build_l1():
    nc = bass.Bass("TRN2", target_bir_lowering=False)
    ap = {}
    for nm, shp, dt in [
        ("xin", [S, D], F32), ("wqT", [D, D], F32), ("wkT", [D, D], F32),
        ("wvT", [D, D], F32), ("woT", [D, D], F32),
        ("bq_c", [128, KC], F32), ("bk_c", [128, KC], F32),
        ("bv_ext", [1, NH * 65], F32), ("bo_row", [1, D], F32),
        ("c128", [128, S], F32), ("s128", [128, S], F32),
        ("p_t", [128, 128], BF16), ("tri", [128, 128], BF16),
        ("ones", [1, 128], F32), ("ident", [128, 128], F32),
    ]:
        ap[nm] = nc.dram_tensor(nm, shp, dt, kind="ExternalInput").ap()
    ap["x2_out"] = nc.dram_tensor("x2_out", [S, D], F32, kind="ExternalOutput").ap()
    ap["mr_out"] = nc.dram_tensor("mr_out", [128, 2 * TC], F32, kind="ExternalOutput").ap()
    ap["mx_out"] = nc.dram_tensor("mx_out", [1, 1], F32, kind="ExternalOutput").ap()
    ap["y1d"] = nc.dram_tensor("y1d", [S, D], BF16, kind="Internal").ap()

    es = contextlib.ExitStack()
    with es:
        E = es.enter_context
        p = Prog(nc, es)
        ld = p.sem("ld"); sd = p.sem("sd"); st = p.sem("st")
        sd2 = p.sem("sd2"); sd2x = p.sem("sd2x")
        tv = p.sem("tv"); ta = p.sem("ta"); tt = p.sem("tt"); tg = p.sem("tg")

        xb = [E(nc.sbuf_tensor(f"xb{i}", [128, D], F32)) for i in range(2)]
        s1_t = E(nc.sbuf_tensor("s1_t", [128, TC], F32))
        s2_t = E(nc.sbuf_tensor("s2_t", [128, TC], F32))
        m_t = E(nc.sbuf_tensor("m_t", [128, TC], F32))
        e2_t = E(nc.sbuf_tensor("e2_t", [128, TC], F32))
        var_t = E(nc.sbuf_tensor("var_t", [128, TC], F32))
        std_t = E(nc.sbuf_tensor("std_t", [128, TC], F32))
        r_t = E(nc.sbuf_tensor("r_t", [128, TC], F32))
        mrn_t = E(nc.sbuf_tensor("mrn_t", [128, TC], F32))
        y1b = [E(nc.sbuf_tensor(f"y1b{i}", [128, D], BF16)) for i in range(2)]
        y1T = E(nc.sbuf_tensor("y1T", [128, KC, S], BF16, is_transpose_dest=True))
        wT = [E(nc.sbuf_tensor(f"wT{i}", [128, KC, D], BF16)) for i in range(2)]
        wstage = [E(nc.sbuf_tensor(f"wstage{i}", [128, 512], F32)) for i in range(3)]
        wsem = [p.sem(f"wsem{i}") for i in range(3)]
        csem = [p.sem(f"csem{i}") for i in range(3)]
        bq_s = E(nc.sbuf_tensor("bq_s", [128, KC], F32))
        bk_s = E(nc.sbuf_tensor("bk_s", [128, KC], F32))

        cb = E(nc.sbuf_tensor("cb", [128, S], BF16))
        sb = E(nc.sbuf_tensor("sb", [128, S], BF16))
        pt_s = E(nc.sbuf_tensor("pt_s", [128, 128], BF16))
        tri_s = E(nc.sbuf_tensor("tri_s", [128, 128], BF16))
        ones_s = E(nc.sbuf_tensor("ones_s", [1, 128], F32))
        id_s = E(nc.sbuf_tensor("id_s", [128, 128], F32))
        bv_rep = E(nc.sbuf_tensor("bv_rep", [128, NH * 65], F32))
        bo_rep = bv_rep
        q_fm = E(nc.sbuf_tensor("q_fm", [128, KC, S], BF16))
        k_fm = E(nc.sbuf_tensor("k_fm", [128, KC, S], BF16))
        v_ext = E(nc.sbuf_tensor("v_ext", [128, TC * NH * 65], BF16))
        qlin0 = E(nc.sbuf_tensor("qlin0", [128, 512], BF16))
        qlin = [qlin0, qlin0]
        t1b0 = E(nc.sbuf_tensor("t1b0", [128, 512], BF16))
        t1b = [t1b0, t1b0]
        pT = [E(nc.sbuf_tensor(f"pT{i}", [128, 1024], BF16)) for i in range(2)]
        recip2x = E(nc.sbuf_tensor("recip2x", [1, 512], F32))
        rep_bf0 = E(nc.sbuf_tensor("rep_bf0", [128, 512], BF16))
        rep_bf = [rep_bf0, rep_bf0]
        t4b0 = E(nc.sbuf_tensor("t4b0", [128, 512], F32))
        t4b = [t4b0, t4b0]
        x2b0 = E(nc.sbuf_tensor("x2b0", [128, D], F32))
        x2b = [x2b0, x2b0]
        mx_t = E(nc.sbuf_tensor("mx_t", [128, TC], F32))
        mn_t = E(nc.sbuf_tensor("mn_t", [128, TC], F32))
        mab = E(nc.sbuf_tensor("mab", [128, TC], F32))
        mab2 = e2_t
        mabc = E(nc.sbuf_tensor("mabc", [128, 1], F32))

        eps_t = mabc

        ps_s = [E(nc.psum_tensor(f"ps_s{i}", [128, 1024], F32)) for i in range(3)]
        ps_av = [E(nc.psum_tensor(f"ps_av{i}", [128, 512], F32)) for i in range(2)]
        ps_r = [ps_av[0], ps_av[1]]
        free = {f"s{i}": None for i in range(6)}
        free.update({"av0": None, "av1": None, "r0": None, "r1": None})

        d_bq = p.dma("sync", bq_s[:, :], ap["bq_c"])
        d_bk = p.dma("sync", bk_s[:, :], ap["bk_c"])
        d_eps = p.do("vector", lambda e: e.memset(eps_t[:, :], 1e-5), sem=tv)
        d_pt = p.dma("sync", pt_s[:, :], ap["p_t"])
        d_tri = p.dma("sync", tri_s[:, :], ap["tri"])
        d_ones = p.dma("sync", ones_s[:, :], ap["ones"])
        d_id = p.dma("sync", id_s[:, :], ap["ident"])
        cs_prev = [None, None, None]
        ii = 0
        for srcn, dst in (("c128", cb), ("s128", sb)):
            for j in range(4):
                s = ii % 3
                deps = [cs_prev[s]] if cs_prev[s] else []
                dd = p.dma("scalar", wstage[s][:, :],
                           ap[srcn][:, j * 512:(j + 1) * 512],
                           deps=deps, sem=csem[s])
                cs_prev[s] = p.do("gpsimd", (lambda e, dst=dst, j=j, s=s:
                    e.tensor_copy(dst[:, j * 512:(j + 1) * 512],
                                  wstage[s][:, :])), deps=[dd], sem=tg)
                ii += 1
        cs_done = cs_prev[(ii - 1) % 3]

        def bcast_row(row_dram, dst, width, deps):
            dep = None
            off = 0
            prev_mm = None
            while off < width:
                w = min(512, width - off)
                dps = list(deps) + ([prev_mm] if prev_mm else [])
                dr = p.dma("sync", recip2x[0:1, 0:w], row_dram[0:1, off:off + w],
                           deps=dps, sem=p.sem("row_sem"))
                dps = [dr, d_ones]
                if free["r0"]:
                    dps.append(free["r0"])
                mm = p.do("tensor", (lambda e, w=w: e.matmul(
                    ps_r[0][:, 0:w], ones_s[0:1, :], recip2x[0:1, 0:w],
                    start=True, stop=True)), deps=dps, sem=tt)
                prev_mm = mm
                dep = p.do("vector", (lambda e, o=off, w=w, d=dst: e.tensor_copy(
                    d[:, o:o + w], ps_r[0][:, 0:w])), deps=[mm], sem=tv)
                free["r0"] = dep
                off += w
            return dep

        d_bvrep = bcast_row(ap["bv_ext"], bv_rep, NH * 65, [])
        d_borep = None

        class WLoader:
            def __init__(self, src_name, dst, start_deps=()):
                self.src = src_name
                self.dst = dst
                self.start = list(start_deps)
                self.i = 0
                self.done = [None] * KC

            def step(self):
                if self.i >= 2 * KC:
                    return
                kc, h = divmod(self.i, 2)
                s = self.i % 3
                deps = list(self.start) if self.i < 3 else []
                if cs_prev[s]:
                    deps.append(cs_prev[s])
                dl = p.dma("scalar", wstage[s][:, :],
                           ap[self.src][kc * 128:(kc + 1) * 128,
                                        h * 512:(h + 1) * 512],
                           deps=deps, sem=wsem[s])
                cs_prev[s] = p.do("gpsimd", (lambda e, kc=kc, h=h,
                                             d=self.dst, s=s:
                    e.tensor_copy(d[:, kc, h * 512:(h + 1) * 512],
                                  wstage[s][:, :])), deps=[dl], sem=tg)
                if h == 1:
                    self.done[kc] = cs_prev[s]
                self.i += 1

            def finish(self):
                while self.i < 2 * KC:
                    self.step()
                return self.done


        wl_q = WLoader("wqT", wT[0])
        wl_k = WLoader("wkT", wT[1])
        feeders = [wl_q, wl_k]

        s2b_t = mx_t
        d_y1T = [[None, None] for _ in range(KC)]
        td = p.sem("td"); td2 = p.sem("td2")
        xfree = [[], []]
        ybfree = [None, None]
        y1_store = []
        prev_stats = None
        for tc in range(TC):
            b = tc % 2
            dx = p.dma("sync", xb[b][:, :], ap["xin"][tc * 128:(tc + 1) * 128, :],
                       deps=xfree[b], sem=(ld if b == 0 else sd2x))
            rv = p.do("vector", (lambda e, tc=tc, b=b: e.reduce_sum(
                s1_t[:, tc:tc + 1], xb[b][:, :], AX.X)), deps=[dx], sem=tv)
            ra1 = p.do("scalar", (lambda e, tc=tc, b=b: e.activation(
                t4b[0][:, :], xb[b][:, 0:512], AF.Square,
                accum_out=s2_t[:, tc:tc + 1])), deps=[dx], sem=ta)
            ra2 = p.do("scalar", (lambda e, tc=tc, b=b: e.activation(
                t4b[1][:, :], xb[b][:, 512:1024], AF.Square,
                accum_out=s2b_t[:, tc:tc + 1])), deps=[dx, ra1], sem=ta)
            v0 = p.do("vector", (lambda e, tc=tc: e.tensor_add(
                s2_t[:, tc:tc + 1], s2_t[:, tc:tc + 1], s2b_t[:, tc:tc + 1])),
                deps=[ra1, ra2], sem=tv)
            v1 = p.do("vector", (lambda e, tc=tc: e.tensor_scalar(
                m_t[:, tc:tc + 1], s1_t[:, tc:tc + 1], 1.0 / D, None, ALU.mult)),
                deps=[rv], sem=tv)
            v2 = p.do("vector", (lambda e, tc=tc: e.tensor_scalar(
                s2b_t[:, tc:tc + 1], s2_t[:, tc:tc + 1], 1.0 / D, None, ALU.mult)),
                deps=[v0], sem=tv)
            v3 = p.do("vector", (lambda e, tc=tc: e.tensor_tensor(
                var_t[:, tc:tc + 1], m_t[:, tc:tc + 1], m_t[:, tc:tc + 1],
                ALU.mult)), deps=[v1], sem=tv)
            v4 = p.do("vector", (lambda e, tc=tc: e.tensor_sub(
                var_t[:, tc:tc + 1], s2b_t[:, tc:tc + 1], var_t[:, tc:tc + 1])),
                deps=[v2, v3], sem=tv)
            a1 = p.do("scalar", (lambda e, tc=tc: e.activation(
                var_t[:, tc:tc + 1], var_t[:, tc:tc + 1], AF.Sqrt,
                bias=eps_t[:, 0:1])), deps=[v4, d_eps], sem=ta)
            v5 = p.do("vector", (lambda e, tc=tc: e.reciprocal(
                r_t[:, tc:tc + 1], var_t[:, tc:tc + 1])), deps=[a1], sem=tv)
            v6 = p.do("vector", (lambda e, tc=tc: e.scalar_tensor_tensor(
                mrn_t[:, tc:tc + 1], m_t[:, tc:tc + 1], -1.0, r_t[:, tc:tc + 1],
                ALU.mult, ALU.mult)), deps=[v5], sem=tv)
            deps = [dx, v6]
            if ybfree[b]:
                deps.append(ybfree[b])
            an = p.do("scalar", (lambda e, tc=tc, b=b: e.activation(
                y1b[b][:, :], xb[b][:, :], AF.Identity,
                bias=mrn_t[:, tc:tc + 1], scale=r_t[:, tc:tc + 1])),
                deps=deps, sem=ta)
            sy = p.dma("scalar", ap["y1d"][tc * 128:(tc + 1) * 128, :], y1b[b][:, :],
                       deps=[an], sem=(sd if tc < 8 else sd2))
            ybfree[b] = sy
            y1_store.append(sy)
            xfree[b] = [an]
            for _ in range(2):
                f = feeders[0] if feeders[0].i < 2 * KC else feeders[1]
                f.step()

        for c in range(KC):
            d_y1T[c][0] = p.dma("sync", y1T[:, c, 0:1024],
                                ap["y1d"][0:1024, c * 128:(c + 1) * 128],
                                deps=[y1_store[7]], sem=td, transpose=True)
        for c in range(KC):
            d_y1T[c][1] = p.dma("scalar", y1T[:, c, 1024:2048],
                                ap["y1d"][1024:2048, c * 128:(c + 1) * 128],
                                deps=[y1_store[-1]], sem=td2, transpose=True)

        wq_done = wl_q.finish()
        wk_done = wl_k.finish()

        def proj_fm(wsb, w_done, d_bias, bias_sb, dst_fm, seed=None):
            prev_rp = [d_bvrep, d_bvrep]
            prev_ql = [seed, None]
            prev_rpq = [None, None]
            prev_t1 = [None, None]
            prev_t2 = [None, None]
            last = None
            for c in range(KC):
                for tb in range(QB):
                    g4 = (c * QB + tb) % NSLOT
                    ib2 = (c * QB + tb) % 2
                    skey = f"s{g4}"
                    ps_view = ps_s[g4 // 2][:, (g4 % 2) * 512:(g4 % 2) * 512 + 512]
                    mm = None
                    for kc in range(KC):
                        deps = [d_y1T[kc][tb // 2], w_done[kc]]
                        if kc == 0 and free.get(skey):
                            deps.append(free[skey])
                        mm = p.do("tensor", (lambda e, c=c, tb=tb, kc=kc, pv=ps_view, w=wsb:
                            e.matmul(pv,
                                     w[:, kc, c * 128:(c + 1) * 128],
                                     y1T[:, kc, tb * 512:(tb + 1) * 512],
                                     start=(kc == 0), stop=(kc == KC - 1))),
                            deps=deps, sem=tt)
                    deps = [mm, d_bias]
                    for dq in prev_ql:
                        if dq:
                            deps.append(dq)
                    for dq in prev_rpq:
                        if dq:
                            deps.append(dq)
                    aq = p.do("scalar", (lambda e, c=c, ib2=ib2, pv=ps_view, bs=bias_sb:
                        e.activation(qlin[ib2][:, :], pv,
                                     AF.Identity, bias=bs[:, c:c + 1], scale=1.0)),
                        deps=deps, sem=ta)
                    free[skey] = aq
                    deps = [aq, d_pt]
                    rkey = f"r{ib2}"
                    if free[rkey]:
                        deps.append(free[rkey])
                    if prev_rp[ib2]:
                        deps.append(prev_rp[ib2])
                    rp = p.do("tensor", (lambda e, ib2=ib2: e.matmul(
                        ps_r[ib2][:, :], pt_s[:, :], qlin[ib2][:, :],
                        start=True, stop=True)), deps=deps, sem=tt)
                    prev_rpq[ib2] = rp
                    deps = [aq, cs_done]
                    if prev_t1[ib2]:
                        deps.append(prev_t1[ib2])
                    dt1 = [d for d in prev_t1 if d]
                    w1 = p.do("vector", (lambda e, tb=tb, ib2=ib2: e.tensor_mul(
                        t1b[ib2][:, :], qlin[ib2][:, :],
                        cb[:, tb * 512:(tb + 1) * 512])), deps=deps + dt1, sem=tv)
                    prev_ql[ib2] = w1
                    deps = [rp, cs_done]
                    if prev_t2[ib2]:
                        deps.append(prev_t2[ib2])
                    w2 = p.do("vector", (lambda e, c=c, tb=tb, ib2=ib2, d=dst_fm:
                        e.scalar_tensor_tensor(
                            d[:, c, tb * 512:(tb + 1) * 512], ps_r[ib2][:, :], 1.0,
                            sb[:, tb * 512:(tb + 1) * 512], ALU.mult, ALU.mult)),
                        deps=deps, sem=tv)
                    free[rkey] = w2
                    w3 = p.do("vector", (lambda e, c=c, tb=tb, ib2=ib2, d=dst_fm:
                        e.tensor_add(d[:, c, tb * 512:(tb + 1) * 512],
                                     t1b[ib2][:, :],
                                     d[:, c, tb * 512:(tb + 1) * 512])),
                        deps=[w1, w2], sem=tv)
                    prev_t1[ib2] = w3
                    prev_t2[ib2] = w3
                    prev_rp[ib2] = None
                    last = w3
            return last

        q_all = proj_fm(wT[0], wq_done, d_bq, bq_s, q_fm)
        wv_done = WLoader("wvT", wT[0], [q_all]).finish()
        k_all = proj_fm(wT[1], wk_done, d_bk, bk_s, k_fm, seed=q_all)
        wo_done = WLoader("woT", wT[1], [k_all]).finish()

        ones_set = []
        for tc in range(TC):
            base = tc * NH * 65
            ones_set.append(p.do("gpsimd", (lambda e, base=base: e.memset(
                v_ext[:, base:base + NH * 65].rearrange(
                    "p (h j) -> p h j", h=NH)[:, :, 64:65], 1.0)), sem=tg))

        v_all = None
        for tc in range(TC):
            for nh2 in range(2):
                g4 = (tc * 2 + nh2) % NSLOT
                skey = f"s{g4}"
                ps_view = ps_s[g4 // 2][:, (g4 % 2) * 512:(g4 % 2) * 512 + 512]
                mm = None
                for kc in range(KC):
                    deps = [d_y1T[kc][tc // 8], wv_done[kc]]
                    if kc == 0 and free.get(skey):
                        deps.append(free[skey])
                    mm = p.do("tensor", (lambda e, tc=tc, nh2=nh2, kc=kc, pv=ps_view:
                        e.matmul(pv,
                                 y1T[:, kc, tc * 128:(tc + 1) * 128],
                                 wT[0][:, kc, nh2 * 512:(nh2 + 1) * 512],
                                 start=(kc == 0), stop=(kc == KC - 1))),
                        deps=deps, sem=tt)
                base = tc * NH * 65 + nh2 * 8 * 65
                ev = p.do("vector", (lambda e, base=base, pv=ps_view, nh2=nh2:
                    e.scalar_tensor_tensor(
                        v_ext[:, base:base + 8 * 65].rearrange(
                            "p (h j) -> p h j", h=8)[:, :, 0:64],
                        pv.rearrange("p (h j) -> p h j", h=8),
                        0.0,
                        bv_rep[:, nh2 * 8 * 65:(nh2 + 1) * 8 * 65].rearrange(
                            "p (h j) -> p h j", h=8)[:, :, 0:64],
                        ALU.add, ALU.add)),
                    deps=[mm, d_bvrep, ones_set[tc]], sem=tv)
                free[skey] = ev
                v_all = ev

        d_borep = bcast_row(ap["bo_row"], bo_rep, D, [v_all])

        pV = y1T
        pT_free = [None, None]
        pV_last = None
        free_ev = {"av0": None, "av1": None}
        free["av0"] = free["r0"]
        free["av1"] = free["r1"]
        qb_div_done = {}

        red2 = []
        x2st = []
        xfree3 = [None, None]
        x2free = [[], []]
        t4_free = [None, None]
        ostate = {}

        def o_group(tc, nh2, pv_dep):
            b = tc % 2
            if nh2 == 0:
                deps = [pv_dep] + ([xfree3[b]] if xfree3[b] else [])
                deps += x2free[0] + x2free[1]
                ostate[tc] = (p.dma("scalar", xb[b][:, :],
                                    ap["xin"][tc * 128:(tc + 1) * 128, :],
                                    deps=deps, sem=sd), [])
            dxi, adds = ostate[tc]
            g4 = (tc * 2 + nh2) % NSLOT
            skey = f"s{g4}"
            ps_view = ps_s[g4 // 2][:, (g4 % 2) * 512:(g4 % 2) * 512 + 512]
            mm = None
            for c in range(KC):
                deps = [pv_dep, wo_done[c]]
                if c == 0 and free.get(skey):
                    deps.append(free[skey])
                mm = p.do("tensor", (lambda e, tc=tc, nh2=nh2, c=c, pv=ps_view:
                    e.matmul(pv,
                             pV[:, c, tc * 128:(tc + 1) * 128],
                             wT[1][:, c, nh2 * 512:(nh2 + 1) * 512],
                             start=(c == 0), stop=(c == KC - 1))),
                    deps=deps, sem=tt)
            deps = [mm, d_borep]
            if t4_free[0]:
                deps.append(t4_free[0])
            if t4_free[1]:
                deps.append(t4_free[1])
            e1 = p.do("vector", (lambda e, nh2=nh2, ob2=nh2, pv=ps_view:
                e.scalar_tensor_tensor(
                    t4b[ob2][:, :], pv, 0.0,
                    bo_rep[:, nh2 * 512:(nh2 + 1) * 512], ALU.add, ALU.add)),
                deps=deps, sem=tv)
            free[skey] = e1
            deps = [e1, dxi] + (x2free[0] + x2free[1] if nh2 == 0 else [])
            e2 = p.do("gpsimd", (lambda e, nh2=nh2, ob2=nh2, b=b: e.tensor_add(
                x2b[b][:, nh2 * 512:(nh2 + 1) * 512], t4b[ob2][:, :],
                xb[b][:, nh2 * 512:(nh2 + 1) * 512])), deps=deps, sem=tg)
            t4_free[nh2] = e2
            adds.append(e2)
            if nh2 == 1:
                xfree3[b] = adds[-1]
                sx = p.dma("gpsimd", ap["x2_out"][tc * 128:(tc + 1) * 128, :],
                           x2b[b][:, :], deps=adds, sem=st)
                x2st.append(sx)
                rv = p.do("vector", (lambda e, tc=tc, b=b: e.reduce_sum(
                    s1_t[:, tc:tc + 1], x2b[b][:, :], AX.X)), deps=adds, sem=tv)
                ra = p.do("scalar", (lambda e, tc=tc, b=b: e.activation(
                    xb[b][:, :], x2b[b][:, :], AF.Square,
                    accum_out=s2_t[:, tc:tc + 1])), deps=adds, sem=ta)
                rmx = p.do("vector", (lambda e, tc=tc, b=b: e.reduce_max(
                    mx_t[:, tc:tc + 1], x2b[b][:, :], AX.X)), deps=adds, sem=tv)
                rmn = p.do("vector", (lambda e, tc=tc, b=b: e.tensor_reduce(
                    mn_t[:, tc:tc + 1], x2b[b][:, :], AX.X, ALU.min)),
                    deps=adds, sem=tv)
                red2.extend([rv, ra, rmx, rmn])
                x2free[b] = [sx, ra, rmx, rmn]
                xfree3[b] = ra


        for qb in range(QB):
            for hp in range(NH // 2):
                c = hp
                nkt = 4 * qb + 4
                av_mm = [None, None]
                mask_by_kt = {}
                exp_by_kt = {}
                for kt in range(nkt + 1):
                    if kt < nkt:
                        sbk = kt % 3
                        skey = f"s{2 * sbk}"
                        skey2 = f"s{2 * sbk + 1}"
                        deps = [q_all, k_all]
                        if free.get(skey):
                            deps.append(free[skey])
                        if free.get(skey2):
                            deps.append(free[skey2])
                        mm = None
                        for half in range(2):
                            mm = p.do("tensor", (lambda e, c=c, qb=qb, kt=kt,
                                                 half=half, sbk=sbk: e.matmul(
                                ps_s[sbk][:, half * 512:(half + 1) * 512],
                                k_fm[half * 64:(half + 1) * 64, c,
                                     kt * 128:(kt + 1) * 128],
                                q_fm[half * 64:(half + 1) * 64, c,
                                     qb * 512:(qb + 1) * 512],
                                start=True, stop=True,
                                tile_position=(64 * half, 0))), deps=deps, sem=tt)
                            deps = []
                        ptk = kt % 2
                        edeps = [mm]
                        if pT_free[ptk]:
                            edeps.append(pT_free[ptk])
                        ex = p.do("scalar", (lambda e, sbk=sbk, ptk=ptk: e.activation(
                            pT[ptk][:, :], ps_s[sbk][:, :], AF.Exp, scale=0.125)),
                            deps=edeps, sem=ta)
                        free[skey] = ex
                        free[skey2] = ex
                        mask_deps = [ex]
                        if kt >= 4 * qb:
                            ktl = kt - 4 * qb
                            if ktl > 0:
                                z0 = p.do("vector", (lambda e, ptk=ptk, ktl=ktl:
                                    e.memset(pT[ptk][:, 0:128 * ktl], 0.0)),
                                    deps=[ex], sem=tv)
                                z1 = p.do("vector", (lambda e, ptk=ptk, ktl=ktl:
                                    e.memset(pT[ptk][:, 512:512 + 128 * ktl], 0.0)),
                                    deps=[ex], sem=tv)
                                mask_deps += [z0, z1]
                            for half in range(2):
                                tm = p.do("vector", (lambda e, ptk=ptk, ktl=ktl,
                                                     half=half: e.tensor_mul(
                                    pT[ptk][:, half * 512 + 128 * ktl:
                                            half * 512 + 128 * (ktl + 1)],
                                    pT[ptk][:, half * 512 + 128 * ktl:
                                            half * 512 + 128 * (ktl + 1)],
                                    tri_s[:, :])), deps=[ex, d_tri], sem=tv)
                                mask_deps.append(tm)
                        mask_by_kt[kt] = mask_deps
                    ka = kt - 1
                    if ka >= 0:
                        ptka = ka % 2
                        for half in range(2):
                            h = 2 * hp + half
                            akey = f"av{half}"
                            adeps = list(mask_by_kt[ka]) + [v_all]
                            if ka == 0 and free[akey]:
                                adeps.append(free[akey])
                            if ka == 0 and free_ev[akey]:
                                adeps.append(free_ev[akey])
                            av_mm[half] = p.do("tensor", (lambda e, h=h, ka=ka,
                                                          ptka=ptka, half=half,
                                                          last=(ka == nkt - 1):
                                e.matmul(ps_av[half][0:65, :],
                                         v_ext[:, ka * NH * 65 + h * 65:
                                               ka * NH * 65 + h * 65 + 65],
                                         pT[ptka][:, half * 512:(half + 1) * 512],
                                         start=(ka == 0), stop=last)),
                                deps=adeps, sem=tt)
                        pT_free[ptka] = av_mm[1]
                rb = hp % 2
                ev0 = p.do("vector", (lambda e, c=c, qb=qb: e.tensor_copy(
                    pV[0:64, c, qb * 512:(qb + 1) * 512], ps_av[0][0:64, :])),
                    deps=[av_mm[0], q_all, k_all, v_all], sem=tv)
                ev1 = p.do("vector", (lambda e, c=c, qb=qb: e.tensor_copy(
                    pV[64:128, c, qb * 512:(qb + 1) * 512], ps_av[1][0:64, :])),
                    deps=[av_mm[1]], sem=tv)
                rdep = [free["r0"], free["r1"]]
                rkey = f"r{rb}"
                ln0 = p.do("scalar", (lambda e: e.activation(
                    recip2x[0:1, :], ps_av[0][64:65, :], AF.Ln)),
                    deps=[av_mm[0]] + [d for d in rdep if d], sem=ta)
                rc0 = p.do("scalar", (lambda e: e.activation(
                    recip2x[0:1, :], recip2x[0:1, :], AF.Exp, scale=-1.0)),
                    deps=[ln0], sem=ta)
                free["av0"] = rc0
                free_ev["av0"] = ev0
                dbk = nkt % 3
                dkey0, dkey1 = f"s{2 * dbk}", f"s{2 * dbk + 1}"
                deps = [rc0, d_ones]
                if free.get(dkey0):
                    deps.append(free[dkey0])
                b1 = p.do("tensor", (lambda e, dbk=dbk: e.matmul(
                    ps_s[dbk][0:64, 0:512], ones_s[0:1, 0:64], recip2x[0:1, :],
                    start=True, stop=True, tile_position=(0, 0))),
                    deps=deps, sem=tt)
                ln1 = p.do("scalar", (lambda e: e.activation(
                    recip2x[0:1, :], ps_av[1][64:65, :], AF.Ln)),
                    deps=[av_mm[1], b1], sem=ta)
                rc1 = p.do("scalar", (lambda e: e.activation(
                    recip2x[0:1, :], recip2x[0:1, :], AF.Exp, scale=-1.0)),
                    deps=[ln1], sem=ta)
                free["av1"] = rc1
                free_ev["av1"] = ev1
                b2 = p.do("tensor", (lambda e, dbk=dbk: e.matmul(
                    ps_s[dbk][64:128, 0:512], ones_s[0:1, 64:128], recip2x[0:1, :],
                    start=True, stop=True, tile_position=(0, 64))),
                    deps=[rc1], sem=tt)
                deps = [b1, b2] + ([pV_last] if pV_last else [])
                re_ = p.do("vector", (lambda e, dbk=dbk: e.tensor_copy(
                    rep_bf[0][:, :], ps_s[dbk][:, 0:512])), deps=deps, sem=tv)
                free[dkey0] = re_
                dv = p.do("vector", (lambda e, c=c, qb=qb: e.tensor_mul(
                    pV[:, c, qb * 512:(qb + 1) * 512],
                    pV[:, c, qb * 512:(qb + 1) * 512],
                    rep_bf[0][:, :])), deps=[re_, ev0, ev1], sem=tv)
                pV_last = dv
                qb_div_done[qb] = dv
                if qb >= 1:
                    tcx = 4 * (qb - 1) + hp // 2
                    o_group(tcx, hp % 2, qb_div_done[qb - 1])

        for hp in range(NH // 2):
            tcx = 12 + hp // 2
            o_group(tcx, hp % 2, qb_div_done[3])

        u1 = p.do("vector", lambda e: e.tensor_scalar(
            m_t[:, :], s1_t[:, :], 1.0 / D, None, ALU.mult), deps=red2, sem=tv)
        u2 = p.do("vector", lambda e: e.tensor_scalar(
            e2_t[:, :], s2_t[:, :], 1.0 / D, None, ALU.mult), deps=red2, sem=tv)
        u3 = p.do("vector", lambda e: e.tensor_tensor(
            var_t[:, :], m_t[:, :], m_t[:, :], ALU.mult), deps=[u1], sem=tv)
        u4 = p.do("vector", lambda e: e.tensor_sub(
            var_t[:, :], e2_t[:, :], var_t[:, :]), deps=[u2, u3], sem=tv)
        u5 = p.do("scalar", lambda e: e.activation(
            std_t[:, :], var_t[:, :], AF.Sqrt, bias=eps_t[:, 0:1]),
            deps=[u4, d_eps], sem=ta)
        u6 = p.do("vector", lambda e: e.reciprocal(
            r_t[:, :], std_t[:, :]), deps=[u5, y1_store[-1]], sem=tv)
        w1 = p.do("vector", lambda e: e.tensor_sub(
            mab[:, :], mx_t[:, :], m_t[:, :]), deps=[u1] + red2, sem=tv)
        w2 = p.do("vector", lambda e: e.tensor_sub(
            mab2[:, :], m_t[:, :], mn_t[:, :]), deps=[u1] + red2, sem=tv)
        w3 = p.do("vector", lambda e: e.tensor_max(
            mab[:, :], mab[:, :], mab2[:, :]), deps=[w1, w2], sem=tv)
        w4 = p.do("vector", lambda e: e.tensor_mul(
            mab[:, :], mab[:, :], r_t[:, :]), deps=[w3, u6], sem=tv)
        w5 = p.do("vector", lambda e: e.reduce_max(
            mabc[:, :], mab[:, :], AX.X), deps=[w4], sem=tv)
        deps = [w5, d_id, free["av0"], free["av1"]]
        tpd = p.do("tensor", lambda e: e.transpose(
            ps_av[0][0:1, 0:128], mabc[:, :], id_s[:, :]), deps=deps, sem=tt)
        w6 = p.do("vector", lambda e: e.reduce_max(
            mabc[0:1, 0:1], ps_av[0][0:1, 0:128], AX.X), deps=[tpd, w5], sem=tv)

        s_mr = p.dma("gpsimd", ap["mr_out"][:, 0:TC], m_t[:, :],
                     deps=[u1, w1, w2], sem=st)
        s_mr2 = p.dma("gpsimd", ap["mr_out"][:, TC:2 * TC], r_t[:, :],
                      deps=[u6, w4], sem=st)
        s_mx = p.dma("gpsimd", ap["mx_out"], mabc[0:1, 0:1], deps=[w6], sem=st)
        p.do("gpsimd", lambda e: e.nop(), deps=[s_mr, s_mx] + x2st)

        p.build()
    return nc


def build_l2():
    nc = bass.Bass("TRN2", target_bir_lowering=False)
    ap = {}
    for nm, shp, dt in [
        ("x2", [S, D], F32), ("mr", [128, 2 * TC], F32), ("mxall", [1, NCORES], F32),
        ("clip_x", [1, 1], F32), ("wg", [DI, D], F32), ("wu", [DI, D], F32),
        ("ones", [1, 128], F32), ("ident", [128, 128], F32),
    ]:
        ap[nm] = nc.dram_tensor(nm, shp, dt, kind="ExternalInput").ap()
    ap["hd_out"] = nc.dram_tensor("hd_out", [DI, S], F16, kind="ExternalOutput").ap()
    ap["mh_out"] = nc.dram_tensor("mh_out", [1, 1], F32, kind="ExternalOutput").ap()

    es = contextlib.ExitStack()
    with es:
        E = es.enter_context
        p = Prog(nc, es)
        ld = p.sem("ld"); sd = p.sem("sd"); st = p.sem("st")
        tv = p.sem("tv"); ta = p.sem("ta"); tt = p.sem("tt"); tg = p.sem("tg")

        mr_s = E(nc.sbuf_tensor("mr_s", [128, 2 * TC], F32))
        mx_s = E(nc.sbuf_tensor("mx_s", [1, NCORES], F32))
        clip_s = E(nc.sbuf_tensor("clip_s", [1, 1], F32))
        ones_s = E(nc.sbuf_tensor("ones_s", [1, 128], F32))
        id_s = E(nc.sbuf_tensor("id_s", [128, 128], F32))
        gm = E(nc.sbuf_tensor("gm", [1, 1], F32))
        sg = E(nc.sbuf_tensor("sg", [1, 1], F32))
        sxi = E(nc.sbuf_tensor("sxi", [1, 1], F32))
        invb = E(nc.sbuf_tensor("invb", [128, 1], F32))
        sxb = E(nc.sbuf_tensor("sxb", [128, 1], F32))
        A_t = E(nc.sbuf_tensor("A_t", [128, TC], F32))
        B_t = E(nc.sbuf_tensor("B_t", [128, TC], F32))
        x2b = [E(nc.sbuf_tensor(f"x2b{i}", [128, D], F32)) for i in range(2)]
        ub0 = E(nc.sbuf_tensor("ub0", [128, D], F16))
        ub = [ub0, ub0]

        ib0_ = E(nc.sbuf_tensor("ib0", [128, D], I32))
        ib_ = [ib0_, ib0_]
        qb0_ = E(nc.sbuf_tensor("qb0", [128, D], F32))
        qb_ = [qb0_, qb0_]
        ub32 = [qb0_, qb0_]
        xq_t = [E(nc.sbuf_tensor(f"xq_t{i}", [128, D], BF16)) for i in range(2)]
        xq_fm = E(nc.sbuf_tensor("xq_fm", [128, KC, S], BF16, is_transpose_dest=True))
        wst = [E(nc.sbuf_tensor(f"wst{i}", [128, D], F32)) for i in range(2)]
        wstB = [E(nc.sbuf_tensor(f"wstB{i}", [128, D], F32)) for i in range(2)]
        wqiB = [E(nc.sbuf_tensor(f"wqiB{i}", [128, D], BF16)) for i in range(2)]
        ivB = [E(nc.sbuf_tensor(f"ivB{i}", [128, 1], F32)) for i in range(2)]
        wub = E(nc.sbuf_tensor("wub", [128, D], F16))
        wub32 = E(nc.sbuf_tensor("wub32", [128, D], F32))
        wib = E(nc.sbuf_tensor("wib", [128, D], I32))
        wqi = [E(nc.sbuf_tensor(f"wqi{i}", [128, D], BF16)) for i in range(2)]
        am_g = E(nc.sbuf_tensor("am_g", [128, IC], F32))
        am_u = E(nc.sbuf_tensor("am_u", [128, IC], F32))
        iv = [E(nc.sbuf_tensor(f"iv{i}", [128, 1], F32)) for i in range(2)]
        wgT = E(nc.sbuf_tensor("wgT", [128, KC, DI], BF16, is_transpose_dest=True))
        wuT = E(nc.sbuf_tensor("wuT", [128, KC, DI], BF16, is_transpose_dest=True))
        sc_g = E(nc.sbuf_tensor("sc_g", [128, IC], F32))
        sc_u = E(nc.sbuf_tensor("sc_u", [128, IC], F32))
        silu_b0 = E(nc.sbuf_tensor("silu_b0", [128, 512], F16))
        silu_b = [silu_b0, silu_b0]
        hp_b0 = E(nc.sbuf_tensor("hp_b0", [128, S], F16))
        hp_b = [hp_b0, hp_b0]
        hmax = E(nc.sbuf_tensor("hmax", [128, 1], F32))
        hm1 = E(nc.sbuf_tensor("hm1", [128, 1], F32))
        mhv = E(nc.sbuf_tensor("mhv", [1, 1], F32))
        roff_t = E(nc.sbuf_tensor("roff_t", [128, 1], F32))
        ps_a = [E(nc.psum_tensor(f"ps_a{i}", [128, 512], F32)) for i in range(2)]
        ps_b = [E(nc.psum_tensor(f"ps_b{i}", [128, 512], F32)) for i in range(2)]
        ps_c = E(nc.psum_tensor("ps_c", [128, 128], F32))

        d_mr = p.dma("sync", mr_s[:, :], ap["mr"])
        d_mx = p.dma("sync", mx_s[:, :], ap["mxall"])
        d_roff = p.do("vector", lambda e: e.memset(roff_t[:, :], MAG), sem=tv)
        d_cl = p.dma("sync", clip_s[:, :], ap["clip_x"])
        d_on = p.dma("sync", ones_s[:, :], ap["ones"])
        d_id = p.dma("sync", id_s[:, :], ap["ident"])

        g1 = p.do("vector", lambda e: e.reduce_max(gm[0:1, :], mx_s[0:1, :], AX.X),
                  deps=[d_mx], sem=tv)
        g2 = p.do("scalar", lambda e: e.activation(sg[0:1, :], clip_s[0:1, :],
                                                   AF.Sigmoid), deps=[d_cl], sem=ta)
        g3 = p.do("vector", lambda e: e.tensor_mul(gm[0:1, :], gm[0:1, :], sg[0:1, :]),
                  deps=[g1, g2], sem=tv)
        g4 = p.do("vector", lambda e: e.tensor_scalar(
            gm[0:1, :], gm[0:1, :], 1.0 / 127.0, None, ALU.mult), deps=[g3], sem=tv)
        g5 = p.do("vector", lambda e: e.reciprocal(sxi[0:1, :], gm[0:1, :]),
                  deps=[g4], sem=tv)
        g6 = p.do("tensor", lambda e: e.matmul(
            ps_c[:, 0:1], ones_s[0:1, :], sxi[0:1, 0:1], start=True, stop=True),
            deps=[g5, d_on], sem=tt)
        g7 = p.do("vector", lambda e: e.tensor_copy(invb[:, :], ps_c[:, 0:1]),
                  deps=[g6], sem=tv)
        g8 = p.do("tensor", lambda e: e.matmul(
            ps_c[:, 0:1], ones_s[0:1, :], gm[0:1, 0:1], start=True, stop=True),
            deps=[g7], sem=tt)
        g9 = p.do("vector", lambda e: e.tensor_copy(sxb[:, :], ps_c[:, 0:1]),
                  deps=[g8], sem=tv)
        a1 = p.do("vector", lambda e: e.tensor_scalar(
            A_t[:, :], mr_s[:, TC:2 * TC], invb[:, 0:1], None, ALU.mult),
            deps=[g7, d_mr], sem=tv)
        a2 = p.do("vector", lambda e: e.tensor_mul(
            B_t[:, :], mr_s[:, 0:TC], A_t[:, :]), deps=[a1], sem=tv)
        a3 = p.do("vector", lambda e: e.tensor_scalar(
            B_t[:, :], B_t[:, :], -1.0, MAG, ALU.mult, ALU.add), deps=[a2], sem=tv)

        ev_x = [[None] * KC for _ in range(TC)]
        xfree = [None, None]
        trf = [None, None]
        txq = [p.sem("txq0"), p.sem("txq1")]
        for tc in range(TC):
            b = tc % 2
            deps = [xfree[b]] if xfree[b] else []
            dx = p.dma("sync" if b == 0 else "scalar", x2b[b][:, :],
                       ap["x2"][tc * 128:(tc + 1) * 128, :],
                       deps=deps, sem=(ld if b == 0 else sd))
            q1 = p.do("vector", (lambda e, tc=tc, b=b: e.tensor_scalar(
                ub[0][:, :], x2b[b][:, :], A_t[:, tc:tc + 1], B_t[:, tc:tc + 1],
                ALU.mult, ALU.add)), deps=[dx, a3], sem=tv)
            xfree[b] = q1
            q1b = p.do("vector", (lambda e: e.tensor_scalar(
                ub32[0][:, :], ub[0][:, :], C2, None, ALU.add)),
                deps=[q1], sem=tv)
            q2 = p.do("vector", (lambda e: e.tensor_copy(
                ib_[0][:, :], ub32[0][:, :])), deps=[q1b], sem=tv)
            q3 = p.do("vector", (lambda e: e.tensor_scalar(
                qb_[0][:, :], ib_[0][:, :], 16384.0, 127.0,
                ALU.subtract, ALU.min)), deps=[q2], sem=tv)
            deps = [q3] + ([trf[b]] if trf[b] else [])
            q4 = p.do("vector", (lambda e, b=b: e.tensor_scalar(
                xq_t[b][:, :], qb_[0][:, :], -128.0, None, ALU.max)),
                deps=deps, sem=tv)
            for c in range(KC):
                tdma = p.dma("sync" if b == 0 else "scalar",
                             xq_fm[:, c, tc * 128:(tc + 1) * 128],
                             xq_t[b][:, c * 128:(c + 1) * 128],
                             deps=[q4], sem=txq[b], transpose=True)
                ev_x[tc][c] = tdma
                trf[b] = tdma

        class WPrep:
            def __init__(self, srcn, am_sb, dstT, q, qsem_name, wst_, wqi_, iv_):
                self.src = ap[srcn]
                self.am = am_sb
                self.dstT = dstT
                self.q = q
                self.qsem = p.sem(qsem_name)
                self.tsem = p.sem(qsem_name + "_t")
                self.wst = wst_
                self.wqi = wqi_
                self.iv = iv_
                self.lfree = [[], []]
                self.q1f = [None, None]
                self.q3f = [None, None]
                self.ev = [[None] * KC for _ in range(IC)]
                self.r_last = None

            def chunk(self, icx):
                b = icx % 2
                dl = p.dma(self.q, self.wst[b][:, :],
                           self.src[icx * 128:(icx + 1) * 128, :],
                           deps=self.lfree[b], sem=self.qsem)
                r1 = p.do("vector", (lambda e, icx=icx, b=b, s=self:
                    e.reduce_max(s.am[:, icx:icx + 1], s.wst[b][:, :], AX.X,
                                 apply_absolute_value=True)),
                    deps=[dl], sem=tv)
                self.r_last = r1
                r2 = p.do("vector", (lambda e, b=b, icx=icx, s=self:
                    e.reciprocal(s.iv[b][:, :], s.am[:, icx:icx + 1])),
                    deps=[r1] + ([self.q1f[b]] if self.q1f[b] else []), sem=tv)
                r3 = p.do("vector", (lambda e, b=b, s=self: e.tensor_scalar(
                    s.iv[b][:, :], s.iv[b][:, :], 127.0, None, ALU.mult)),
                    deps=[r2], sem=tv)
                q1 = p.do("vector", (lambda e, b=b, s=self: e.tensor_scalar(
                    wub[:, :], s.wst[b][:, :], s.iv[b][:, 0:1], MAG,
                    ALU.mult, ALU.add)), deps=[r3], sem=tv)
                self.lfree[b] = [r1, q1]
                self.q1f[b] = q1
                q1b = p.do("vector", (lambda e: e.tensor_scalar(
                    wub32[:, :], wub[:, :], C2, None, ALU.add)),
                    deps=[q1], sem=tv)
                q2 = p.do("vector", (lambda e: e.tensor_copy(
                    wib[:, :], wub32[:, :])), deps=[q1b], sem=tv)
                q3 = p.do("vector", (lambda e, b=b, s=self: e.tensor_scalar(
                    s.wqi[b][:, :], wib[:, :], 16384.0, None, ALU.subtract)),
                    deps=[q2] + ([self.q3f[b]] if self.q3f[b] else []), sem=tv)
                last_t = None
                for c in range(KC):
                    last_t = p.dma(self.q,
                                   self.dstT[:, c, icx * 128:(icx + 1) * 128],
                                   self.wqi[b][:, c * 128:(c + 1) * 128],
                                   deps=[q3], sem=self.tsem, transpose=True)
                    self.ev[icx][c] = last_t
                self.q3f[b] = last_t

        wpg = WPrep("wg", am_g, wgT, "sync", "wl_g", wst, wqi, iv)
        wpu = WPrep("wu", am_u, wuT, "scalar", "wl_u", wstB, wqiB, ivB)
        for icx in range(IC):
            wpg.chunk(icx)
            wpu.chunk(icx)
        wg_ev, wg_end = wpg.ev, wpg.r_last
        wu_ev, wu_end = wpu.ev, wpu.r_last

        c1 = p.do("vector", lambda e: e.tensor_scalar(
            sc_g[:, :], am_g[:, :], sxb[:, 0:1], 1.0 / 127.0, ALU.mult, ALU.mult),
            deps=[g9, wg_end], sem=tv)
        c2 = p.do("vector", lambda e: e.tensor_scalar(
            sc_u[:, :], am_u[:, :], sxb[:, 0:1], 1.0 / 127.0, ALU.mult, ALU.mult),
            deps=[g9, wu_end], sem=tv)
        hm0 = p.do("vector", lambda e: e.memset(hmax[:, :], 0.0), sem=tv)

        h_st = []
        psa_free = [None, None]
        psb_free = [None, None]
        silu_free = [None, None]
        hp_free = [[], []]
        hmax_dep = hm0
        for icx in range(IC):
            for tb in range(QB):
                pb2 = (icx * QB + tb) % 2
                mmg = None
                for kc in range(KC):
                    deps = [wg_ev[icx][kc]] + [ev_x[tb * 4 + k][kc]
                                               for k in range(4)]
                    if kc == 0 and psa_free[pb2]:
                        deps.append(psa_free[pb2])
                    mmg = p.do("tensor", (lambda e, icx=icx, tb=tb, kc=kc, pb2=pb2:
                        e.matmul(ps_a[pb2][:, :],
                                 wgT[:, kc, icx * 128:(icx + 1) * 128],
                                 xq_fm[:, kc, tb * 512:(tb + 1) * 512],
                                 start=(kc == 0), stop=(kc == KC - 1))),
                        deps=deps, sem=tt)
                deps = [mmg, c1]
                for sfd in silu_free:
                    if sfd:
                        deps.append(sfd)
                sl = p.do("scalar", (lambda e, icx=icx, pb2=pb2: e.activation(
                    silu_b[pb2][:, :], ps_a[pb2][:, :], AF.Silu,
                    scale=sc_g[:, icx:icx + 1])), deps=deps, sem=ta)
                psa_free[pb2] = sl
                mmu = None
                for kc in range(KC):
                    deps = [wu_ev[icx][kc]] + [ev_x[tb * 4 + k][kc]
                                               for k in range(4)]
                    if kc == 0 and psb_free[pb2]:
                        deps.append(psb_free[pb2])
                    mmu = p.do("tensor", (lambda e, icx=icx, tb=tb, kc=kc, pb2=pb2:
                        e.matmul(ps_b[pb2][:, :],
                                 wuT[:, kc, icx * 128:(icx + 1) * 128],
                                 xq_fm[:, kc, tb * 512:(tb + 1) * 512],
                                 start=(kc == 0), stop=(kc == KC - 1))),
                        deps=deps, sem=tt)
                hb2 = 0
                deps = [mmu, sl, c2] + (hp_free[hb2] if tb == 0 else [])
                hh = p.do("vector", (lambda e, icx=icx, tb=tb, pb2=pb2, hb2=hb2:
                    e.scalar_tensor_tensor(
                        hp_b[hb2][:, tb * 512:(tb + 1) * 512],
                        ps_b[pb2][:, :], sc_u[:, icx:icx + 1],
                        silu_b[pb2][:, :], ALU.mult, ALU.mult)), deps=deps, sem=tv)
                psb_free[pb2] = hh
                silu_free[pb2] = hh
                if tb == QB - 1:
                    sh = p.dma("gpsimd", ap["hd_out"][icx * 128:(icx + 1) * 128, :],
                               hp_b[hb2][:, :], deps=[hh], sem=st)
                    rm = p.do("vector", (lambda e, hb2=hb2: e.reduce_max(
                        hm1[:, :], hp_b[hb2][:, :], AX.X,
                        apply_absolute_value=True)),
                        deps=[hh, hmax_dep], sem=tv)
                    hmax_dep = p.do("vector", (lambda e: e.tensor_max(
                        hmax[:, :], hmax[:, :], hm1[:, :])), deps=[rm], sem=tv)
                    hp_free[hb2] = [sh, rm]
                    h_st.append(sh)
        tpd = p.do("tensor", lambda e: e.transpose(
            ps_c[0:1, 0:128], hmax[:, :], id_s[:, :]),
            deps=[hmax_dep, d_id, g9], sem=tt)
        f1 = p.do("vector", lambda e: e.reduce_max(
            mhv[0:1, :], ps_c[0:1, 0:128], AX.X), deps=[tpd], sem=tv)
        s_mh = p.dma("gpsimd", ap["mh_out"], mhv[0:1, :], deps=[f1], sem=st)
        p.do("gpsimd", lambda e: e.nop(), deps=[s_mh] + h_st[-2:])

        p.build()
    return nc


def build_l3():
    nc = bass.Bass("TRN2", target_bir_lowering=False)
    ap = {}
    for nm, shp, dt in [
        ("hd", [DI, S], F16), ("x2", [S, D], F32), ("mhall", [1, NCORES], F32),
        ("clip_d", [1, 1], F32), ("wd", [D, DI], F32), ("ones", [1, 128], F32),
    ]:
        ap[nm] = nc.dram_tensor(nm, shp, dt, kind="ExternalInput").ap()
    ap["out"] = nc.dram_tensor("out", [S, D], F32, kind="ExternalOutput").ap()
    ap["wdi"] = nc.dram_tensor("wdi", [D, DI], BF16, kind="Internal").ap()
    ap["swd"] = nc.dram_tensor("swd", [1, D], F32, kind="Internal").ap()

    es = contextlib.ExitStack()
    with es:
        E = es.enter_context
        p = Prog(nc, es)
        ld = p.sem("ld"); sd = p.sem("sd"); st = p.sem("st")
        tv = p.sem("tv"); ta = p.sem("ta"); tt = p.sem("tt"); tg = p.sem("tg")

        mh_s = E(nc.sbuf_tensor("mh_s", [1, NCORES], F32))
        clip_s = E(nc.sbuf_tensor("clip_s", [1, 1], F32))
        ones_s = E(nc.sbuf_tensor("ones_s", [1, 128], F32))
        gm = E(nc.sbuf_tensor("gm", [1, 1], F32))
        sg = E(nc.sbuf_tensor("sg", [1, 1], F32))
        shi = E(nc.sbuf_tensor("shi", [1, 1], F32))
        k1 = E(nc.sbuf_tensor("k1", [1, 1], F32))
        invb = E(nc.sbuf_tensor("invb", [128, 1], F32))
        wst = E(nc.sbuf_tensor("wst", [128, DI], F32))
        am_d = E(nc.sbuf_tensor("am_d", [128, KC], F32))
        iv = [E(nc.sbuf_tensor(f"iv{i}", [128, 1], F32)) for i in range(2)]
        wub = E(nc.sbuf_tensor("wub", [128, DI], F16))
        wub32 = E(nc.sbuf_tensor("wub32", [128, DI], F32))
        wib = E(nc.sbuf_tensor("wib", [128, DI], I32))
        wqi = E(nc.sbuf_tensor("wqi", [128, DI], BF16))
        wdT = E(nc.sbuf_tensor("wdT", [128, IC, D], BF16, is_transpose_dest=True))
        srow = E(nc.sbuf_tensor("srow", [1, D], F32))
        srep = E(nc.sbuf_tensor("srep", [128, D], F32))
        hb = [E(nc.sbuf_tensor(f"hb{i}", [128, 1024], F16)) for i in range(2)]
        uq = [E(nc.sbuf_tensor(f"uq{i}", [128, 1024], F16)) for i in range(2)]
        uq32 = [E(nc.sbuf_tensor(f"uq32{i}", [128, 1024], F32)) for i in range(2)]
        iq = [E(nc.sbuf_tensor(f"iq{i}", [128, 1024], I32)) for i in range(2)]
        qq = [E(nc.sbuf_tensor(f"qq{i}", [128, 1024], F32)) for i in range(2)]
        hq = E(nc.sbuf_tensor("hq", [128, IC * 1024], BF16))
        x2b = [E(nc.sbuf_tensor(f"x2b{i}", [128, D], F32)) for i in range(2)]
        t5b = [E(nc.sbuf_tensor(f"t5b{i}", [128, 512], F32)) for i in range(2)]
        roff_t = E(nc.sbuf_tensor("roff_t", [128, 1], F32))
        ob = [E(nc.sbuf_tensor(f"ob{i}", [128, D], F32)) for i in range(2)]
        ps_d = [E(nc.psum_tensor(f"ps_d{i}", [128, 512], F32)) for i in range(2)]
        ps_c = E(nc.psum_tensor("ps_c", [128, 512], F32))

        d_mh = p.dma("sync", mh_s[:, :], ap["mhall"])
        d_cl = p.dma("sync", clip_s[:, :], ap["clip_d"])
        d_roff = p.do("vector", lambda e: e.memset(roff_t[:, :], MAG), sem=tv)
        d_on = p.dma("sync", ones_s[:, :], ap["ones"])

        g1 = p.do("vector", lambda e: e.reduce_max(gm[0:1, :], mh_s[0:1, :], AX.X),
                  deps=[d_mh], sem=tv)
        g2 = p.do("scalar", lambda e: e.activation(sg[0:1, :], clip_s[0:1, :],
                                                   AF.Sigmoid), deps=[d_cl], sem=ta)
        g3 = p.do("vector", lambda e: e.tensor_mul(gm[0:1, :], gm[0:1, :], sg[0:1, :]),
                  deps=[g1, g2], sem=tv)
        g4 = p.do("vector", lambda e: e.tensor_scalar(
            gm[0:1, :], gm[0:1, :], 1.0 / 127.0, None, ALU.mult), deps=[g3], sem=tv)
        g5 = p.do("vector", lambda e: e.reciprocal(shi[0:1, :], gm[0:1, :]),
                  deps=[g4], sem=tv)
        g6 = p.do("vector", lambda e: e.tensor_scalar(
            k1[0:1, :], gm[0:1, :], 1.0 / 127.0, None, ALU.mult), deps=[g4], sem=tv)
        g7 = p.do("tensor", lambda e: e.matmul(
            ps_c[:, 0:1], ones_s[0:1, :], shi[0:1, 0:1], start=True, stop=True),
            deps=[g5, d_on], sem=tt)
        g8 = p.do("vector", lambda e: e.tensor_copy(invb[:, :], ps_c[:, 0:1]),
                  deps=[g7], sem=tv)

        wd_fq = []
        chain = None
        for c in range(KC):
            deps = [chain] if chain else []
            dl = p.dma("gpsimd", wst[:, :], ap["wd"][c * 128:(c + 1) * 128, :],
                       deps=deps)
            r1 = p.do("vector", (lambda e, c=c: e.reduce_max(
                am_d[:, c:c + 1], wst[:, :], AX.X, apply_absolute_value=True)),
                deps=[dl], sem=tv)
            r2 = p.do("vector", (lambda e, c=c: e.reciprocal(
                iv[0][:, :], am_d[:, c:c + 1])), deps=[r1] + ([chain] if chain else []),
                sem=tv)
            r3 = p.do("vector", lambda e: e.tensor_scalar(
                iv[0][:, :], iv[0][:, :], 127.0, None, ALU.mult), deps=[r2], sem=tv)
            q1 = p.do("vector", lambda e: e.tensor_scalar(
                wub[:, :], wst[:, :], iv[0][:, 0:1], MAG, ALU.mult, ALU.add),
                deps=[r3], sem=tv)
            q1b = p.do("vector", lambda e: e.tensor_scalar(
                wub32[:, :], wub[:, :], C2, None, ALU.add), deps=[q1], sem=tv)
            q2 = p.do("vector", lambda e: e.tensor_copy(wib[:, :], wub32[:, :]),
                      deps=[q1b], sem=tv)
            q3 = p.do("vector", lambda e: e.tensor_scalar(
                wqi[:, :], wib[:, :], 16384.0, None, ALU.subtract), deps=[q2], sem=tv)
            sw = p.dma("gpsimd", ap["wdi"][c * 128:(c + 1) * 128, :], wqi[:, :],
                       deps=[q3], sem=st)
            wd_fq.append(sw)
            chain = sw
        wd_T = []
        for icx in range(IC):
            wd_T.append(p.dma("scalar", wdT[:, icx, :],
                              ap["wdi"][:, icx * 128:(icx + 1) * 128],
                              deps=wd_fq, sem=sd, transpose=True))

        sc_st = []
        for c in range(KC):
            sc_st.append(p.dma(
                "gpsimd",
                ap["swd"][0:1, c * 128:(c + 1) * 128].rearrange("one p -> p one"),
                am_d[:, c:c + 1], deps=wd_fq, sem=st))
        d_sr = p.dma("sync", srow[0:1, :], ap["swd"], deps=sc_st)
        z1 = p.do("vector", lambda e: e.tensor_scalar(
            srow[0:1, :], srow[0:1, :], k1[0:1, 0:1], None, ALU.mult),
            deps=[d_sr, g6], sem=tv)
        z2 = p.do("tensor", lambda e: e.matmul(
            ps_c[:, :], ones_s[0:1, :], srow[0:1, 0:512], start=True, stop=True),
            deps=[z1, d_on, g8], sem=tt)
        z3 = p.do("vector", lambda e: e.tensor_copy(srep[:, 0:512], ps_c[:, :]),
                  deps=[z2], sem=tv)
        z4 = p.do("tensor", lambda e: e.matmul(
            ps_c[:, :], ones_s[0:1, :], srow[0:1, 512:1024], start=True, stop=True),
            deps=[z3], sem=tt)
        z5 = p.do("vector", lambda e: e.tensor_copy(srep[:, 512:1024], ps_c[:, :]),
                  deps=[z4], sem=tv)

        out_st = []
        psd_free = [None, None]
        t5_free = [None, None]
        obx_free = [None, None]
        x2_free = [None, None]
        uq_free = [None, None]
        uq32_free = [None, None]
        iq_free = [None, None]
        qq_free = [None, None]
        hb_free = [None, None]
        last_mm = None
        for halfS in range(2):
            hq_done = [None] * IC
            for icx in range(IC):
                b = icx % 2
                deps = [hb_free[b]] if hb_free[b] else []
                dl = p.dma("sync", hb[b][:, :],
                           ap["hd"][icx * 128:(icx + 1) * 128,
                                    halfS * 1024:(halfS + 1) * 1024], deps=deps)
                deps = [dl, g8, d_roff] + ([uq_free[b]] if uq_free[b] else [])
                q1 = p.do("vector", (lambda e, b=b: e.tensor_scalar(
                    uq[b][:, :], hb[b][:, :], invb[:, 0:1], MAG,
                    ALU.mult, ALU.add)), deps=deps, sem=tv)
                hb_free[b] = q1
                deps = [q1] + ([uq32_free[b]] if uq32_free[b] else [])
                q1b = p.do("vector", (lambda e, b=b: e.tensor_scalar(
                    uq32[b][:, :], uq[b][:, :], C2, None, ALU.add)), deps=deps, sem=tv)
                uq_free[b] = q1b
                deps = [q1b] + ([iq_free[b]] if iq_free[b] else [])
                q2 = p.do("vector", (lambda e, b=b: e.tensor_copy(
                    iq[b][:, :], uq32[b][:, :])), deps=deps, sem=tv)
                uq32_free[b] = q2
                deps = [q2] + ([qq_free[b]] if qq_free[b] else [])
                q3 = p.do("vector", (lambda e, b=b: e.tensor_scalar(
                    qq[b][:, :], iq[b][:, :], 16384.0, 127.0, ALU.subtract,
                    ALU.min)), deps=deps, sem=tv)
                iq_free[b] = q3
                deps = [q3] + ([last_mm] if last_mm else [])
                q4 = p.do("vector", (lambda e, icx=icx, b=b: e.tensor_scalar(
                    hq[:, icx * 1024:(icx + 1) * 1024], qq[b][:, :],
                    -128.0, None, ALU.max)), deps=deps, sem=tv)
                qq_free[b] = q4
                hq_done[icx] = q4
            for tch in range(TC // 2):
                tc = halfS * (TC // 2) + tch
                b = tc % 2
                deps = [x2_free[b]] if x2_free[b] else []
                dx = p.dma("scalar", x2b[b][:, :],
                           ap["x2"][tc * 128:(tc + 1) * 128, :], deps=deps, sem=sd)
                odep = []
                for nh2 in range(2):
                    pb2 = nh2
                    mm = None
                    for icx in range(IC):
                        deps = [hq_done[icx], wd_T[icx]]
                        if icx == 0 and psd_free[pb2]:
                            deps.append(psd_free[pb2])
                        mm = p.do("tensor", (lambda e, tch=tch, nh2=nh2, icx=icx,
                                             pb2=pb2: e.matmul(
                            ps_d[pb2][:, :],
                            hq[:, icx * 1024 + tch * 128:icx * 1024 + (tch + 1) * 128],
                            wdT[:, icx, nh2 * 512:(nh2 + 1) * 512],
                            start=(icx == 0), stop=(icx == IC - 1))),
                            deps=deps, sem=tt)
                        last_mm = mm
                    deps = [mm, z3, z5]
                    if t5_free[pb2]:
                        deps.append(t5_free[pb2])
                    e1 = p.do("vector", (lambda e, nh2=nh2, pb2=pb2: e.tensor_mul(
                        t5b[pb2][:, :], ps_d[pb2][:, :],
                        srep[:, nh2 * 512:(nh2 + 1) * 512])), deps=deps, sem=tv)
                    psd_free[pb2] = e1
                    deps = [e1, dx]
                    if obx_free[b]:
                        deps.append(obx_free[b])
                    e2 = p.do("gpsimd", (lambda e, nh2=nh2, pb2=pb2, b=b:
                        e.tensor_add(ob[b][:, nh2 * 512:(nh2 + 1) * 512],
                                     t5b[pb2][:, :],
                                     x2b[b][:, nh2 * 512:(nh2 + 1) * 512])),
                        deps=deps, sem=tg)
                    t5_free[pb2] = e2
                    odep.append(e2)
                so = p.dma("gpsimd", ap["out"][tc * 128:(tc + 1) * 128, :],
                           ob[b][:, :], deps=odep, sem=st)
                obx_free[b] = so
                x2_free[b] = so
                out_st.append(so)
        p.do("gpsimd", lambda e: e.nop(), deps=out_st[-2:])

        p.build()
    return nc


_CACHE = {}


def kernel(L0_attn_qkv, L0_attn_Wq, L0_attn_bq, L0_attn_Wk, L0_attn_bk,
           L0_attn_Wv, L0_attn_bv, L0_attn_Wo, L0_attn_bo,
           L0_ffn_clip_x, L0_ffn_clip_down, L0_ffn_w_gate, L0_ffn_w_up,
           L0_ffn_w_down):
    C128, S128, P_T, tri, ones_row, ident = host_consts()
    f32 = np.float32
    wqT = np.ascontiguousarray(np.asarray(L0_attn_Wq, f32).T)
    wkT = np.ascontiguousarray(np.asarray(L0_attn_Wk, f32).T)
    wvT = np.ascontiguousarray(np.asarray(L0_attn_Wv, f32).T)
    woT = np.ascontiguousarray(np.asarray(L0_attn_Wo, f32).T)
    bq_c = np.ascontiguousarray(np.asarray(L0_attn_bq, f32).reshape(KC, 128).T)
    bk_c = np.ascontiguousarray(np.asarray(L0_attn_bk, f32).reshape(KC, 128).T)
    bv = np.asarray(L0_attn_bv, f32)
    bv_ext = np.zeros((1, NH * 65), f32)
    for h in range(NH):
        bv_ext[0, h * 65:h * 65 + 64] = bv[h * 64:(h + 1) * 64]
    bo_row = np.ascontiguousarray(np.asarray(L0_attn_bo, f32).reshape(1, D))
    xin = np.asarray(L0_attn_qkv, f32)
    wg = np.ascontiguousarray(np.asarray(L0_ffn_w_gate, f32))
    wu = np.ascontiguousarray(np.asarray(L0_ffn_w_up, f32))
    wd = np.ascontiguousarray(np.asarray(L0_ffn_w_down, f32))
    clip_x = np.asarray(L0_ffn_clip_x, f32).reshape(1, 1)
    clip_d = np.asarray(L0_ffn_clip_down, f32).reshape(1, 1)

    if "l1" not in _CACHE:
        _CACHE["l1"] = build_l1()
        _CACHE["l2"] = build_l2()
        _CACHE["l3"] = build_l3()
    nc1, nc2, nc3 = _CACHE["l1"], _CACHE["l2"], _CACHE["l3"]

    core_ids = list(range(NCORES))
    in1 = [{"xin": np.ascontiguousarray(xin[b]), "wqT": wqT, "wkT": wkT,
            "wvT": wvT, "woT": woT, "bq_c": bq_c, "bk_c": bk_c,
            "bv_ext": bv_ext, "bo_row": bo_row, "c128": C128, "s128": S128,
            "p_t": P_T, "tri": tri, "ones": ones_row, "ident": ident}
           for b in range(NCORES)]
    r1 = run_bass_kernel_spmd(nc1, in1, core_ids=core_ids).results

    mxall = np.concatenate([np.asarray(r1[b]["mx_out"]).reshape(1, 1)
                            for b in range(NCORES)], axis=1)
    in2 = [{"x2": r1[b]["x2_out"], "mr": r1[b]["mr_out"], "mxall": mxall,
            "clip_x": clip_x, "wg": wg, "wu": wu, "ones": ones_row,
            "ident": ident} for b in range(NCORES)]
    r2 = run_bass_kernel_spmd(nc2, in2, core_ids=core_ids).results

    mhall = np.concatenate([np.asarray(r2[b]["mh_out"]).reshape(1, 1)
                            for b in range(NCORES)], axis=1)
    in3 = [{"hd": r2[b]["hd_out"], "x2": r1[b]["x2_out"], "mhall": mhall,
            "clip_d": clip_d, "wd": wd, "ones": ones_row}
           for b in range(NCORES)]
    r3 = run_bass_kernel_spmd(nc3, in3, core_ids=core_ids).results

    out = np.stack([np.asarray(r3[b]["out"]) for b in range(NCORES)], axis=0)
    return out.astype(np.float32)


# version 13 (final)
# speedup vs baseline: 1.0834x; 1.0834x over previous
import contextlib
import numpy as np
import ml_dtypes
import concourse.bass as bass
import concourse.mybir as mybir
from concourse.bass_utils import run_bass_kernel_spmd

F32 = mybir.dt.float32
F16 = mybir.dt.float16
BF16 = mybir.dt.bfloat16
I32 = mybir.dt.int32
AF = mybir.ActivationFunctionType
ALU = mybir.AluOpType
AX = mybir.AxisListType

NCORES = 8
S = 2048
D = 1024
NH = 16
DH = 64
DI = 2816
TC = S // 128
QB = S // 512
KC = D // 128
IC = DI // 128
MAG = 1536.0
C2 = 14848.498046875

ENGINES = ("sync", "vector", "scalar", "tensor", "gpsimd")
NSLOT = 6


class Sem:
    def __init__(self, nc, es, name):
        self.h = es.enter_context(nc.semaphore(name))
        self.n = 0
        self.is_dma = False

    def bump(self, k=1):
        self.n += k
        return self.n


class Prog:
    def __init__(self, nc, es):
        self.nc = nc
        self.es = es
        self.q = {e: [] for e in ENGINES}
        self.sems = {}
        self.last_wait = {e: {} for e in ENGINES}

    def sem(self, name):
        if name not in self.sems:
            self.sems[name] = Sem(self.nc, self.es, name)
        return self.sems[name]

    def do(self, eng, fn, deps=(), sem=None, k=1):
        lw = self.last_wait[eng]
        for d in deps:
            if d is None:
                continue
            s, v = d
            if s.is_dma:
                v = s.n
            if v <= 0 or lw.get(s, 0) >= v:
                continue
            lw[s] = v
            self.q[eng].append(("w", s.h, v))
        if sem is not None:
            val = sem.bump(k)
            self.q[eng].append(("o", fn, sem.h, k))
            return (sem, val)
        self.q[eng].append(("o", fn, None, 0))
        return None

    def dma(self, eng, out_ap, in_ap, deps=(), sem=None, transpose=False):
        if sem is None:
            sem = self.sem("dma_" + eng)
        sem.is_dma = True
        deps = list(deps) + [(sem, sem.n)]
        if transpose:
            fn = lambda e, o=out_ap, i=in_ap: e.dma_start(out=o, in_=i, transpose=True)
        else:
            fn = lambda e, o=out_ap, i=in_ap: e.dma_start(out=o, in_=i)
        return self.do(eng, fn, deps=deps, sem=sem, k=16)

    def build(self):
        with self.nc.Block() as block:
            for e in ENGINES:
                items = self.q[e]
                if not items:
                    continue

                def emit(eng, items=items):
                    for it in items:
                        if it[0] == "w":
                            eng.wait_ge(it[1], it[2])
                        else:
                            ins = it[1](eng)
                            if it[2] is not None:
                                ins.then_inc(it[2], it[3])

                getattr(block, e)(emit)


def host_consts():
    inv_freq = 1.0 / (10000.0 ** (np.arange(0, DH, 2, dtype=np.float64) / DH))
    t = np.arange(S, dtype=np.float64)
    ang = np.outer(t, inv_freq)
    rows = (np.arange(128) % DH) // 2
    C128 = np.cos(ang).T[rows, :].astype(np.float32)
    S128 = np.sin(ang).T[rows, :].astype(np.float32)
    P = np.zeros((128, 128), np.float32)
    for blk in range(2):
        o = blk * 64
        for i in range(DH // 2):
            P[o + 2 * i, o + 2 * i + 1] = -1.0
            P[o + 2 * i + 1, o + 2 * i] = 1.0
    P_T = np.ascontiguousarray(P.T).astype(ml_dtypes.bfloat16)
    tri = (np.arange(128)[:, None] <= np.arange(128)[None, :]).astype(
        ml_dtypes.bfloat16)
    ones_row = np.ones((1, 128), np.float32)
    ident = np.eye(128, dtype=np.float32)
    return C128, S128, P_T, tri, ones_row, ident


def build_l1():
    nc = bass.Bass("TRN2", target_bir_lowering=False)
    ap = {}
    for nm, shp, dt in [
        ("xin", [S, D], F32), ("wqT", [D, D], F32), ("wkT", [D, D], F32),
        ("wvT", [D, D], F32), ("woT", [D, D], F32),
        ("bq_c", [128, KC], F32), ("bk_c", [128, KC], F32),
        ("bv_ext", [1, NH * 65], F32), ("bo_row", [1, D], F32),
        ("c128", [128, S], F32), ("s128", [128, S], F32),
        ("p_t", [128, 128], BF16), ("tri", [128, 128], BF16),
        ("ones", [1, 128], F32), ("ident", [128, 128], F32),
    ]:
        ap[nm] = nc.dram_tensor(nm, shp, dt, kind="ExternalInput").ap()
    ap["x2_out"] = nc.dram_tensor("x2_out", [S, D], F32, kind="ExternalOutput").ap()
    ap["mr_out"] = nc.dram_tensor("mr_out", [128, 2 * TC], F32, kind="ExternalOutput").ap()
    ap["mx_out"] = nc.dram_tensor("mx_out", [1, 1], F32, kind="ExternalOutput").ap()
    ap["y1d"] = nc.dram_tensor("y1d", [S, D], BF16, kind="Internal").ap()

    es = contextlib.ExitStack()
    with es:
        E = es.enter_context
        p = Prog(nc, es)
        ld = p.sem("ld"); sd = p.sem("sd"); st = p.sem("st")
        sd2 = p.sem("sd2"); sd2x = p.sem("sd2x")
        tv = p.sem("tv"); ta = p.sem("ta"); tt = p.sem("tt"); tg = p.sem("tg")

        xb = [E(nc.sbuf_tensor(f"xb{i}", [128, D], F32)) for i in range(2)]
        s1_t = E(nc.sbuf_tensor("s1_t", [128, TC], F32))
        s2_t = E(nc.sbuf_tensor("s2_t", [128, TC], F32))
        m_t = E(nc.sbuf_tensor("m_t", [128, TC], F32))
        e2_t = E(nc.sbuf_tensor("e2_t", [128, TC], F32))
        var_t = E(nc.sbuf_tensor("var_t", [128, TC], F32))
        std_t = E(nc.sbuf_tensor("std_t", [128, TC], F32))
        r_t = E(nc.sbuf_tensor("r_t", [128, TC], F32))
        mrn_t = E(nc.sbuf_tensor("mrn_t", [128, TC], F32))
        y1b = [E(nc.sbuf_tensor(f"y1b{i}", [128, D], BF16)) for i in range(2)]
        y1T = E(nc.sbuf_tensor("y1T", [128, KC, S], BF16, is_transpose_dest=True))
        wT = [E(nc.sbuf_tensor(f"wT{i}", [128, KC, D], BF16)) for i in range(2)]
        wstage = [E(nc.sbuf_tensor(f"wstage{i}", [128, 512], F32)) for i in range(3)]
        wsem = [p.sem(f"wsem{i}") for i in range(3)]
        csem = [p.sem(f"csem{i}") for i in range(3)]
        bq_s = E(nc.sbuf_tensor("bq_s", [128, KC], F32))
        bk_s = E(nc.sbuf_tensor("bk_s", [128, KC], F32))

        cb = E(nc.sbuf_tensor("cb", [128, S], BF16))
        sb = E(nc.sbuf_tensor("sb", [128, S], BF16))
        pt_s = E(nc.sbuf_tensor("pt_s", [128, 128], BF16))
        tri_s = E(nc.sbuf_tensor("tri_s", [128, 128], BF16))
        ones_s = E(nc.sbuf_tensor("ones_s", [1, 128], F32))
        id_s = E(nc.sbuf_tensor("id_s", [128, 128], F32))
        bv_rep = E(nc.sbuf_tensor("bv_rep", [128, NH * 65], F32))
        bo_rep = bv_rep
        q_fm = E(nc.sbuf_tensor("q_fm", [128, KC, S], BF16))
        k_fm = E(nc.sbuf_tensor("k_fm", [128, KC, S], BF16))
        v_ext = E(nc.sbuf_tensor("v_ext", [128, TC * NH * 65], BF16))
        qlin0 = E(nc.sbuf_tensor("qlin0", [128, 512], BF16))
        qlin = [qlin0, qlin0]
        t1b0 = E(nc.sbuf_tensor("t1b0", [128, 512], BF16))
        t1b = [t1b0, t1b0]
        pT = [E(nc.sbuf_tensor(f"pT{i}", [128, 1024], BF16)) for i in range(2)]
        recip2x = E(nc.sbuf_tensor("recip2x", [1, 512], F32))
        rep_bf0 = E(nc.sbuf_tensor("rep_bf0", [128, 512], BF16))
        rep_bf = [rep_bf0, rep_bf0]
        t4b0 = E(nc.sbuf_tensor("t4b0", [128, 512], F32))
        t4b = [t4b0, t4b0]
        x2b0 = E(nc.sbuf_tensor("x2b0", [128, D], F32))
        x2b = [x2b0, x2b0]
        mx_t = E(nc.sbuf_tensor("mx_t", [128, TC], F32))
        mn_t = E(nc.sbuf_tensor("mn_t", [128, TC], F32))
        mab = E(nc.sbuf_tensor("mab", [128, TC], F32))
        mab2 = e2_t
        mabc = E(nc.sbuf_tensor("mabc", [128, 1], F32))

        eps_t = mabc

        ps_s = [E(nc.psum_tensor(f"ps_s{i}", [128, 1024], F32)) for i in range(3)]
        ps_av = [E(nc.psum_tensor(f"ps_av{i}", [128, 512], F32)) for i in range(2)]
        ps_r = [ps_av[0], ps_av[1]]
        free = {f"s{i}": None for i in range(6)}
        free.update({"av0": None, "av1": None, "r0": None, "r1": None})

        d_bq = p.dma("sync", bq_s[:, :], ap["bq_c"])
        d_bk = p.dma("sync", bk_s[:, :], ap["bk_c"])
        d_eps = p.do("vector", lambda e: e.memset(eps_t[:, :], 1e-5), sem=tv)
        d_pt = p.dma("sync", pt_s[:, :], ap["p_t"])
        d_tri = p.dma("sync", tri_s[:, :], ap["tri"])
        d_ones = p.dma("sync", ones_s[:, :], ap["ones"])
        d_id = p.dma("sync", id_s[:, :], ap["ident"])
        cs_prev = [None, None, None]
        ii = 0
        for srcn, dst in (("c128", cb), ("s128", sb)):
            for j in range(4):
                s = ii % 3
                deps = [cs_prev[s]] if cs_prev[s] else []
                dd = p.dma("scalar", wstage[s][:, :],
                           ap[srcn][:, j * 512:(j + 1) * 512],
                           deps=deps, sem=csem[s])
                cs_prev[s] = p.do("gpsimd", (lambda e, dst=dst, j=j, s=s:
                    e.tensor_copy(dst[:, j * 512:(j + 1) * 512],
                                  wstage[s][:, :])), deps=[dd], sem=tg)
                ii += 1
        cs_done = cs_prev[(ii - 1) % 3]

        def bcast_row(row_dram, dst, width, deps):
            dep = None
            off = 0
            prev_mm = None
            while off < width:
                w = min(512, width - off)
                dps = list(deps) + ([prev_mm] if prev_mm else [])
                dr = p.dma("sync", recip2x[0:1, 0:w], row_dram[0:1, off:off + w],
                           deps=dps, sem=p.sem("row_sem"))
                dps = [dr, d_ones]
                if free["r0"]:
                    dps.append(free["r0"])
                mm = p.do("tensor", (lambda e, w=w: e.matmul(
                    ps_r[0][:, 0:w], ones_s[0:1, :], recip2x[0:1, 0:w],
                    start=True, stop=True)), deps=dps, sem=tt)
                prev_mm = mm
                dep = p.do("vector", (lambda e, o=off, w=w, d=dst: e.tensor_copy(
                    d[:, o:o + w], ps_r[0][:, 0:w])), deps=[mm], sem=tv)
                free["r0"] = dep
                off += w
            return dep

        d_bvrep = bcast_row(ap["bv_ext"], bv_rep, NH * 65, [])
        d_borep = None

        class WLoader:
            def __init__(self, src_name, dst, start_deps=()):
                self.src = src_name
                self.dst = dst
                self.start = list(start_deps)
                self.i = 0
                self.done = [None] * KC

            def step(self):
                if self.i >= 2 * KC:
                    return
                kc, h = divmod(self.i, 2)
                s = self.i % 3
                deps = list(self.start) if self.i < 3 else []
                if cs_prev[s]:
                    deps.append(cs_prev[s])
                dl = p.dma("scalar", wstage[s][:, :],
                           ap[self.src][kc * 128:(kc + 1) * 128,
                                        h * 512:(h + 1) * 512],
                           deps=deps, sem=wsem[s])
                cs_prev[s] = p.do("gpsimd", (lambda e, kc=kc, h=h,
                                             d=self.dst, s=s:
                    e.tensor_copy(d[:, kc, h * 512:(h + 1) * 512],
                                  wstage[s][:, :])), deps=[dl], sem=tg)
                if h == 1:
                    self.done[kc] = cs_prev[s]
                self.i += 1

            def finish(self):
                while self.i < 2 * KC:
                    self.step()
                return self.done


        wl_q = WLoader("wqT", wT[0])
        wl_k = WLoader("wkT", wT[1])
        feeders = [wl_q, wl_k]

        s2b_t = mx_t
        d_y1T = [[None, None] for _ in range(KC)]
        td = p.sem("td"); td2 = p.sem("td2")
        xfree = [[], []]
        ybfree = [None, None]
        y1_store = []
        prev_stats = None
        for tc in range(TC):
            b = tc % 2
            dx = p.dma("sync", xb[b][:, :], ap["xin"][tc * 128:(tc + 1) * 128, :],
                       deps=xfree[b], sem=(ld if b == 0 else sd2x))
            rv = p.do("vector", (lambda e, tc=tc, b=b: e.reduce_sum(
                s1_t[:, tc:tc + 1], xb[b][:, :], AX.X)), deps=[dx], sem=tv)
            ra1 = p.do("scalar", (lambda e, tc=tc, b=b: e.activation(
                t4b[0][:, :], xb[b][:, 0:512], AF.Square,
                accum_out=s2_t[:, tc:tc + 1])), deps=[dx], sem=ta)
            ra2 = p.do("scalar", (lambda e, tc=tc, b=b: e.activation(
                t4b[1][:, :], xb[b][:, 512:1024], AF.Square,
                accum_out=s2b_t[:, tc:tc + 1])), deps=[dx, ra1], sem=ta)
            v0 = p.do("vector", (lambda e, tc=tc: e.tensor_add(
                s2_t[:, tc:tc + 1], s2_t[:, tc:tc + 1], s2b_t[:, tc:tc + 1])),
                deps=[ra1, ra2], sem=tv)
            v1 = p.do("vector", (lambda e, tc=tc: e.tensor_scalar(
                m_t[:, tc:tc + 1], s1_t[:, tc:tc + 1], 1.0 / D, None, ALU.mult)),
                deps=[rv], sem=tv)
            v2 = p.do("vector", (lambda e, tc=tc: e.tensor_scalar(
                s2b_t[:, tc:tc + 1], s2_t[:, tc:tc + 1], 1.0 / D, None, ALU.mult)),
                deps=[v0], sem=tv)
            v3 = p.do("vector", (lambda e, tc=tc: e.tensor_tensor(
                var_t[:, tc:tc + 1], m_t[:, tc:tc + 1], m_t[:, tc:tc + 1],
                ALU.mult)), deps=[v1], sem=tv)
            v4 = p.do("vector", (lambda e, tc=tc: e.tensor_sub(
                var_t[:, tc:tc + 1], s2b_t[:, tc:tc + 1], var_t[:, tc:tc + 1])),
                deps=[v2, v3], sem=tv)
            a1 = p.do("scalar", (lambda e, tc=tc: e.activation(
                var_t[:, tc:tc + 1], var_t[:, tc:tc + 1], AF.Sqrt,
                bias=eps_t[:, 0:1])), deps=[v4, d_eps], sem=ta)
            v5 = p.do("vector", (lambda e, tc=tc: e.reciprocal(
                r_t[:, tc:tc + 1], var_t[:, tc:tc + 1])), deps=[a1], sem=tv)
            v6 = p.do("vector", (lambda e, tc=tc: e.scalar_tensor_tensor(
                mrn_t[:, tc:tc + 1], m_t[:, tc:tc + 1], -1.0, r_t[:, tc:tc + 1],
                ALU.mult, ALU.mult)), deps=[v5], sem=tv)
            deps = [dx, v6]
            if ybfree[b]:
                deps.append(ybfree[b])
            an = p.do("scalar", (lambda e, tc=tc, b=b: e.activation(
                y1b[b][:, :], xb[b][:, :], AF.Identity,
                bias=mrn_t[:, tc:tc + 1], scale=r_t[:, tc:tc + 1])),
                deps=deps, sem=ta)
            sy = p.dma("scalar", ap["y1d"][tc * 128:(tc + 1) * 128, :], y1b[b][:, :],
                       deps=[an], sem=(sd if tc < 8 else sd2))
            ybfree[b] = sy
            y1_store.append(sy)
            xfree[b] = [an]
            for _ in range(2):
                f = feeders[0] if feeders[0].i < 2 * KC else feeders[1]
                f.step()

        for c in range(KC):
            d_y1T[c][0] = p.dma("sync", y1T[:, c, 0:1024],
                                ap["y1d"][0:1024, c * 128:(c + 1) * 128],
                                deps=[y1_store[7]], sem=td, transpose=True)
        for c in range(KC):
            d_y1T[c][1] = p.dma("scalar", y1T[:, c, 1024:2048],
                                ap["y1d"][1024:2048, c * 128:(c + 1) * 128],
                                deps=[y1_store[-1]], sem=td2, transpose=True)

        wq_done = wl_q.finish()
        wk_done = wl_k.finish()

        def proj_fm(wsb, w_done, d_bias, bias_sb, dst_fm, seed=None):
            prev_rp = [d_bvrep, d_bvrep]
            prev_ql = [seed, None]
            prev_rpq = [None, None]
            prev_t1 = [None, None]
            prev_t2 = [None, None]
            last = None
            for c in range(KC):
                for tb in range(QB):
                    g4 = (c * QB + tb) % NSLOT
                    ib2 = (c * QB + tb) % 2
                    skey = f"s{g4}"
                    ps_view = ps_s[g4 // 2][:, (g4 % 2) * 512:(g4 % 2) * 512 + 512]
                    mm = None
                    for kc in range(KC):
                        deps = [d_y1T[kc][tb // 2], w_done[kc]]
                        if kc == 0 and free.get(skey):
                            deps.append(free[skey])
                        mm = p.do("tensor", (lambda e, c=c, tb=tb, kc=kc, pv=ps_view, w=wsb:
                            e.matmul(pv,
                                     w[:, kc, c * 128:(c + 1) * 128],
                                     y1T[:, kc, tb * 512:(tb + 1) * 512],
                                     start=(kc == 0), stop=(kc == KC - 1))),
                            deps=deps, sem=tt)
                    deps = [mm, d_bias]
                    for dq in prev_ql:
                        if dq:
                            deps.append(dq)
                    for dq in prev_rpq:
                        if dq:
                            deps.append(dq)
                    aq = p.do("scalar", (lambda e, c=c, ib2=ib2, pv=ps_view, bs=bias_sb:
                        e.activation(qlin[ib2][:, :], pv,
                                     AF.Identity, bias=bs[:, c:c + 1], scale=1.0)),
                        deps=deps, sem=ta)
                    free[skey] = aq
                    deps = [aq, d_pt]
                    rkey = f"r{ib2}"
                    if free[rkey]:
                        deps.append(free[rkey])
                    if prev_rp[ib2]:
                        deps.append(prev_rp[ib2])
                    rp = p.do("tensor", (lambda e, ib2=ib2: e.matmul(
                        ps_r[ib2][:, :], pt_s[:, :], qlin[ib2][:, :],
                        start=True, stop=True)), deps=deps, sem=tt)
                    prev_rpq[ib2] = rp
                    deps = [aq, cs_done]
                    if prev_t1[ib2]:
                        deps.append(prev_t1[ib2])
                    dt1 = [d for d in prev_t1 if d]
                    w1 = p.do("vector", (lambda e, tb=tb, ib2=ib2: e.tensor_mul(
                        t1b[ib2][:, :], qlin[ib2][:, :],
                        cb[:, tb * 512:(tb + 1) * 512])), deps=deps + dt1, sem=tv)
                    prev_ql[ib2] = w1
                    deps = [rp, cs_done]
                    if prev_t2[ib2]:
                        deps.append(prev_t2[ib2])
                    w2 = p.do("vector", (lambda e, c=c, tb=tb, ib2=ib2, d=dst_fm:
                        e.scalar_tensor_tensor(
                            d[:, c, tb * 512:(tb + 1) * 512], ps_r[ib2][:, :], 1.0,
                            sb[:, tb * 512:(tb + 1) * 512], ALU.mult, ALU.mult)),
                        deps=deps, sem=tv)
                    free[rkey] = w2
                    w3 = p.do("vector", (lambda e, c=c, tb=tb, ib2=ib2, d=dst_fm:
                        e.tensor_add(d[:, c, tb * 512:(tb + 1) * 512],
                                     t1b[ib2][:, :],
                                     d[:, c, tb * 512:(tb + 1) * 512])),
                        deps=[w1, w2], sem=tv)
                    prev_t1[ib2] = w3
                    prev_t2[ib2] = w3
                    prev_rp[ib2] = None
                    last = w3
            return last

        q_all = proj_fm(wT[0], wq_done, d_bq, bq_s, q_fm)
        wv_done = WLoader("wvT", wT[0], [q_all]).finish()
        k_all = proj_fm(wT[1], wk_done, d_bk, bk_s, k_fm, seed=q_all)
        wo_done = WLoader("woT", wT[1], [k_all]).finish()

        ones_set = []
        for tc in range(TC):
            base = tc * NH * 65
            ones_set.append(p.do("gpsimd", (lambda e, base=base: e.memset(
                v_ext[:, base:base + NH * 65].rearrange(
                    "p (h j) -> p h j", h=NH)[:, :, 64:65], 1.0)), sem=tg))

        v_all = None
        for tc in range(TC):
            for nh2 in range(2):
                g4 = (tc * 2 + nh2) % NSLOT
                skey = f"s{g4}"
                ps_view = ps_s[g4 // 2][:, (g4 % 2) * 512:(g4 % 2) * 512 + 512]
                mm = None
                for kc in range(KC):
                    deps = [d_y1T[kc][tc // 8], wv_done[kc]]
                    if kc == 0 and free.get(skey):
                        deps.append(free[skey])
                    mm = p.do("tensor", (lambda e, tc=tc, nh2=nh2, kc=kc, pv=ps_view:
                        e.matmul(pv,
                                 y1T[:, kc, tc * 128:(tc + 1) * 128],
                                 wT[0][:, kc, nh2 * 512:(nh2 + 1) * 512],
                                 start=(kc == 0), stop=(kc == KC - 1))),
                        deps=deps, sem=tt)
                base = tc * NH * 65 + nh2 * 8 * 65
                ev = p.do("vector", (lambda e, base=base, pv=ps_view, nh2=nh2:
                    e.scalar_tensor_tensor(
                        v_ext[:, base:base + 8 * 65].rearrange(
                            "p (h j) -> p h j", h=8)[:, :, 0:64],
                        pv.rearrange("p (h j) -> p h j", h=8),
                        0.0,
                        bv_rep[:, nh2 * 8 * 65:(nh2 + 1) * 8 * 65].rearrange(
                            "p (h j) -> p h j", h=8)[:, :, 0:64],
                        ALU.add, ALU.add)),
                    deps=[mm, d_bvrep, ones_set[tc]], sem=tv)
                free[skey] = ev
                v_all = ev

        d_borep = bcast_row(ap["bo_row"], bo_rep, D, [v_all])

        pV = y1T
        pT_free = [None, None]
        pV_last = None
        free_ev = {"av0": None, "av1": None}
        free["av0"] = free["r0"]
        free["av1"] = free["r1"]
        qb_div_done = {}

        red2 = []
        x2st = []
        xfree3 = [None, None]
        x2free = [[], []]
        t4_free = [None, None]
        ostate = {}

        def o_group(tc, nh2, pv_dep):
            b = tc % 2
            if nh2 == 0:
                deps = [pv_dep] + ([xfree3[b]] if xfree3[b] else [])
                deps += x2free[0] + x2free[1]
                ostate[tc] = (p.dma("scalar", xb[b][:, :],
                                    ap["xin"][tc * 128:(tc + 1) * 128, :],
                                    deps=deps, sem=sd), [])
            dxi, adds = ostate[tc]
            g4 = (tc * 2 + nh2) % NSLOT
            skey = f"s{g4}"
            ps_view = ps_s[g4 // 2][:, (g4 % 2) * 512:(g4 % 2) * 512 + 512]
            mm = None
            for c in range(KC):
                deps = [pv_dep, wo_done[c]]
                if c == 0 and free.get(skey):
                    deps.append(free[skey])
                mm = p.do("tensor", (lambda e, tc=tc, nh2=nh2, c=c, pv=ps_view:
                    e.matmul(pv,
                             pV[:, c, tc * 128:(tc + 1) * 128],
                             wT[1][:, c, nh2 * 512:(nh2 + 1) * 512],
                             start=(c == 0), stop=(c == KC - 1))),
                    deps=deps, sem=tt)
            deps = [mm, d_borep]
            if t4_free[0]:
                deps.append(t4_free[0])
            if t4_free[1]:
                deps.append(t4_free[1])
            e1 = p.do("vector", (lambda e, nh2=nh2, ob2=nh2, pv=ps_view:
                e.scalar_tensor_tensor(
                    t4b[ob2][:, :], pv, 0.0,
                    bo_rep[:, nh2 * 512:(nh2 + 1) * 512], ALU.add, ALU.add)),
                deps=deps, sem=tv)
            free[skey] = e1
            deps = [e1, dxi] + (x2free[0] + x2free[1] if nh2 == 0 else [])
            e2 = p.do("gpsimd", (lambda e, nh2=nh2, ob2=nh2, b=b: e.tensor_add(
                x2b[b][:, nh2 * 512:(nh2 + 1) * 512], t4b[ob2][:, :],
                xb[b][:, nh2 * 512:(nh2 + 1) * 512])), deps=deps, sem=tg)
            t4_free[nh2] = e2
            adds.append(e2)
            if nh2 == 1:
                xfree3[b] = adds[-1]
                sx = p.dma("gpsimd", ap["x2_out"][tc * 128:(tc + 1) * 128, :],
                           x2b[b][:, :], deps=adds, sem=st)
                x2st.append(sx)
                rv = p.do("vector", (lambda e, tc=tc, b=b: e.reduce_sum(
                    s1_t[:, tc:tc + 1], x2b[b][:, :], AX.X)), deps=adds, sem=tv)
                ra = p.do("scalar", (lambda e, tc=tc, b=b: e.activation(
                    xb[b][:, :], x2b[b][:, :], AF.Square,
                    accum_out=s2_t[:, tc:tc + 1])), deps=adds, sem=ta)
                rmx = p.do("vector", (lambda e, tc=tc, b=b: e.reduce_max(
                    mx_t[:, tc:tc + 1], x2b[b][:, :], AX.X)), deps=adds, sem=tv)
                rmn = p.do("vector", (lambda e, tc=tc, b=b: e.tensor_reduce(
                    mn_t[:, tc:tc + 1], x2b[b][:, :], AX.X, ALU.min)),
                    deps=adds, sem=tv)
                red2.extend([rv, ra, rmx, rmn])
                x2free[b] = [sx, ra, rmx, rmn]
                xfree3[b] = ra


        for qb in range(QB):
            for hp in range(NH // 2):
                c = hp
                nkt = 4 * qb + 4
                av_mm = [None, None]
                mask_by_kt = {}
                exp_by_kt = {}
                for kt in range(nkt + 1):
                    if kt < nkt:
                        sbk = kt % 3
                        skey = f"s{2 * sbk}"
                        skey2 = f"s{2 * sbk + 1}"
                        deps = [q_all, k_all]
                        if free.get(skey):
                            deps.append(free[skey])
                        if free.get(skey2):
                            deps.append(free[skey2])
                        mm = None
                        for half in range(2):
                            mm = p.do("tensor", (lambda e, c=c, qb=qb, kt=kt,
                                                 half=half, sbk=sbk: e.matmul(
                                ps_s[sbk][:, half * 512:(half + 1) * 512],
                                k_fm[half * 64:(half + 1) * 64, c,
                                     kt * 128:(kt + 1) * 128],
                                q_fm[half * 64:(half + 1) * 64, c,
                                     qb * 512:(qb + 1) * 512],
                                start=True, stop=True,
                                tile_position=(64 * half, 0))), deps=deps, sem=tt)
                            deps = []
                        ptk = kt % 2
                        edeps = [mm]
                        if pT_free[ptk]:
                            edeps.append(pT_free[ptk])
                        ex = p.do("scalar", (lambda e, sbk=sbk, ptk=ptk: e.activation(
                            pT[ptk][:, :], ps_s[sbk][:, :], AF.Exp, scale=0.125)),
                            deps=edeps, sem=ta)
                        free[skey] = ex
                        free[skey2] = ex
                        mask_deps = [ex]
                        if kt >= 4 * qb:
                            ktl = kt - 4 * qb
                            if ktl > 0:
                                z0 = p.do("vector", (lambda e, ptk=ptk, ktl=ktl:
                                    e.memset(pT[ptk][:, 0:128 * ktl], 0.0)),
                                    deps=[ex], sem=tv)
                                z1 = p.do("vector", (lambda e, ptk=ptk, ktl=ktl:
                                    e.memset(pT[ptk][:, 512:512 + 128 * ktl], 0.0)),
                                    deps=[ex], sem=tv)
                                mask_deps += [z0, z1]
                            for half in range(2):
                                tm = p.do("vector", (lambda e, ptk=ptk, ktl=ktl,
                                                     half=half: e.tensor_mul(
                                    pT[ptk][:, half * 512 + 128 * ktl:
                                            half * 512 + 128 * (ktl + 1)],
                                    pT[ptk][:, half * 512 + 128 * ktl:
                                            half * 512 + 128 * (ktl + 1)],
                                    tri_s[:, :])), deps=[ex, d_tri], sem=tv)
                                mask_deps.append(tm)
                        mask_by_kt[kt] = mask_deps
                    ka = kt - 1
                    if ka >= 0:
                        ptka = ka % 2
                        for half in range(2):
                            h = 2 * hp + half
                            akey = f"av{half}"
                            adeps = list(mask_by_kt[ka]) + [v_all]
                            if ka == 0 and free[akey]:
                                adeps.append(free[akey])
                            if ka == 0 and free_ev[akey]:
                                adeps.append(free_ev[akey])
                            av_mm[half] = p.do("tensor", (lambda e, h=h, ka=ka,
                                                          ptka=ptka, half=half,
                                                          last=(ka == nkt - 1):
                                e.matmul(ps_av[half][0:65, :],
                                         v_ext[:, ka * NH * 65 + h * 65:
                                               ka * NH * 65 + h * 65 + 65],
                                         pT[ptka][:, half * 512:(half + 1) * 512],
                                         start=(ka == 0), stop=last)),
                                deps=adeps, sem=tt)
                        pT_free[ptka] = av_mm[1]
                rb = hp % 2
                ev0 = p.do("vector", (lambda e, c=c, qb=qb: e.tensor_copy(
                    pV[0:64, c, qb * 512:(qb + 1) * 512], ps_av[0][0:64, :])),
                    deps=[av_mm[0], q_all, k_all, v_all], sem=tv)
                ev1 = p.do("vector", (lambda e, c=c, qb=qb: e.tensor_copy(
                    pV[64:128, c, qb * 512:(qb + 1) * 512], ps_av[1][0:64, :])),
                    deps=[av_mm[1]], sem=tv)
                rdep = [free["r0"], free["r1"]]
                rkey = f"r{rb}"
                ln0 = p.do("scalar", (lambda e: e.activation(
                    recip2x[0:1, :], ps_av[0][64:65, :], AF.Ln)),
                    deps=[av_mm[0]] + [d for d in rdep if d], sem=ta)
                rc0 = p.do("scalar", (lambda e: e.activation(
                    recip2x[0:1, :], recip2x[0:1, :], AF.Exp, scale=-1.0)),
                    deps=[ln0], sem=ta)
                free["av0"] = rc0
                free_ev["av0"] = ev0
                dbk = nkt % 3
                dkey0, dkey1 = f"s{2 * dbk}", f"s{2 * dbk + 1}"
                deps = [rc0, d_ones]
                if free.get(dkey0):
                    deps.append(free[dkey0])
                b1 = p.do("tensor", (lambda e, dbk=dbk: e.matmul(
                    ps_s[dbk][0:64, 0:512], ones_s[0:1, 0:64], recip2x[0:1, :],
                    start=True, stop=True, tile_position=(0, 0))),
                    deps=deps, sem=tt)
                ln1 = p.do("scalar", (lambda e: e.activation(
                    recip2x[0:1, :], ps_av[1][64:65, :], AF.Ln)),
                    deps=[av_mm[1], b1], sem=ta)
                rc1 = p.do("scalar", (lambda e: e.activation(
                    recip2x[0:1, :], recip2x[0:1, :], AF.Exp, scale=-1.0)),
                    deps=[ln1], sem=ta)
                free["av1"] = rc1
                free_ev["av1"] = ev1
                b2 = p.do("tensor", (lambda e, dbk=dbk: e.matmul(
                    ps_s[dbk][64:128, 0:512], ones_s[0:1, 64:128], recip2x[0:1, :],
                    start=True, stop=True, tile_position=(0, 64))),
                    deps=[rc1], sem=tt)
                deps = [b1, b2] + ([pV_last] if pV_last else [])
                re_ = p.do("vector", (lambda e, dbk=dbk: e.tensor_copy(
                    rep_bf[0][:, :], ps_s[dbk][:, 0:512])), deps=deps, sem=tv)
                free[dkey0] = re_
                dv = p.do("vector", (lambda e, c=c, qb=qb: e.tensor_mul(
                    pV[:, c, qb * 512:(qb + 1) * 512],
                    pV[:, c, qb * 512:(qb + 1) * 512],
                    rep_bf[0][:, :])), deps=[re_, ev0, ev1], sem=tv)
                pV_last = dv
                qb_div_done[qb] = dv
                if qb >= 1:
                    tcx = 4 * (qb - 1) + hp // 2
                    o_group(tcx, hp % 2, qb_div_done[qb - 1])

        for hp in range(NH // 2):
            tcx = 12 + hp // 2
            o_group(tcx, hp % 2, qb_div_done[3])

        u1 = p.do("vector", lambda e: e.tensor_scalar(
            m_t[:, :], s1_t[:, :], 1.0 / D, None, ALU.mult), deps=red2, sem=tv)
        u2 = p.do("vector", lambda e: e.tensor_scalar(
            e2_t[:, :], s2_t[:, :], 1.0 / D, None, ALU.mult), deps=red2, sem=tv)
        u3 = p.do("vector", lambda e: e.tensor_tensor(
            var_t[:, :], m_t[:, :], m_t[:, :], ALU.mult), deps=[u1], sem=tv)
        u4 = p.do("vector", lambda e: e.tensor_sub(
            var_t[:, :], e2_t[:, :], var_t[:, :]), deps=[u2, u3], sem=tv)
        u5 = p.do("scalar", lambda e: e.activation(
            std_t[:, :], var_t[:, :], AF.Sqrt, bias=eps_t[:, 0:1]),
            deps=[u4, d_eps], sem=ta)
        u6 = p.do("vector", lambda e: e.reciprocal(
            r_t[:, :], std_t[:, :]), deps=[u5, y1_store[-1]], sem=tv)
        w1 = p.do("vector", lambda e: e.tensor_sub(
            mab[:, :], mx_t[:, :], m_t[:, :]), deps=[u1] + red2, sem=tv)
        w2 = p.do("vector", lambda e: e.tensor_sub(
            mab2[:, :], m_t[:, :], mn_t[:, :]), deps=[u1] + red2, sem=tv)
        w3 = p.do("vector", lambda e: e.tensor_max(
            mab[:, :], mab[:, :], mab2[:, :]), deps=[w1, w2], sem=tv)
        w4 = p.do("vector", lambda e: e.tensor_mul(
            mab[:, :], mab[:, :], r_t[:, :]), deps=[w3, u6], sem=tv)
        w5 = p.do("vector", lambda e: e.reduce_max(
            mabc[:, :], mab[:, :], AX.X), deps=[w4], sem=tv)
        deps = [w5, d_id, free["av0"], free["av1"]]
        tpd = p.do("tensor", lambda e: e.transpose(
            ps_av[0][0:1, 0:128], mabc[:, :], id_s[:, :]), deps=deps, sem=tt)
        w6 = p.do("vector", lambda e: e.reduce_max(
            mabc[0:1, 0:1], ps_av[0][0:1, 0:128], AX.X), deps=[tpd, w5], sem=tv)

        s_mr = p.dma("gpsimd", ap["mr_out"][:, 0:TC], m_t[:, :],
                     deps=[u1, w1, w2], sem=st)
        s_mr2 = p.dma("gpsimd", ap["mr_out"][:, TC:2 * TC], r_t[:, :],
                      deps=[u6, w4], sem=st)
        s_mx = p.dma("gpsimd", ap["mx_out"], mabc[0:1, 0:1], deps=[w6], sem=st)
        p.do("gpsimd", lambda e: e.nop(), deps=[s_mr, s_mx] + x2st)

        p.build()
    return nc


def build_l2():
    nc = bass.Bass("TRN2", target_bir_lowering=False)
    ap = {}
    for nm, shp, dt in [
        ("x2", [S, D], F32), ("mr", [128, 2 * TC], F32), ("mxall", [1, NCORES], F32),
        ("clip_x", [1, 1], F32), ("wg", [DI, D], F32), ("wu", [DI, D], F32),
        ("ones", [1, 128], F32), ("ident", [128, 128], F32),
    ]:
        ap[nm] = nc.dram_tensor(nm, shp, dt, kind="ExternalInput").ap()
    ap["hd_out"] = nc.dram_tensor("hd_out", [DI, S], F16, kind="ExternalOutput").ap()
    ap["mh_out"] = nc.dram_tensor("mh_out", [1, 1], F32, kind="ExternalOutput").ap()

    es = contextlib.ExitStack()
    with es:
        E = es.enter_context
        p = Prog(nc, es)
        ld = p.sem("ld"); sd = p.sem("sd"); st = p.sem("st")
        tv = p.sem("tv"); ta = p.sem("ta"); tt = p.sem("tt"); tg = p.sem("tg")

        mr_s = E(nc.sbuf_tensor("mr_s", [128, 2 * TC], F32))
        mx_s = E(nc.sbuf_tensor("mx_s", [1, NCORES], F32))
        clip_s = E(nc.sbuf_tensor("clip_s", [1, 1], F32))
        ones_s = E(nc.sbuf_tensor("ones_s", [1, 128], F32))
        id_s = E(nc.sbuf_tensor("id_s", [128, 128], F32))
        gm = E(nc.sbuf_tensor("gm", [1, 1], F32))
        sg = E(nc.sbuf_tensor("sg", [1, 1], F32))
        sxi = E(nc.sbuf_tensor("sxi", [1, 1], F32))
        invb = E(nc.sbuf_tensor("invb", [128, 1], F32))
        sxb = E(nc.sbuf_tensor("sxb", [128, 1], F32))
        A_t = E(nc.sbuf_tensor("A_t", [128, TC], F32))
        B_t = E(nc.sbuf_tensor("B_t", [128, TC], F32))
        x2b = [E(nc.sbuf_tensor(f"x2b{i}", [128, D], F32)) for i in range(2)]
        ub0 = E(nc.sbuf_tensor("ub0", [128, D], F16))
        ub = [ub0, ub0]

        ib0_ = E(nc.sbuf_tensor("ib0", [128, D], I32))
        ib_ = [ib0_, ib0_]
        qb0_ = E(nc.sbuf_tensor("qb0", [128, D], F32))
        qb_ = [qb0_, qb0_]
        ub32 = [qb0_, qb0_]
        xq_t = [E(nc.sbuf_tensor(f"xq_t{i}", [128, D], BF16)) for i in range(2)]
        xq_fm = E(nc.sbuf_tensor("xq_fm", [128, KC, S], BF16, is_transpose_dest=True))
        wst = [E(nc.sbuf_tensor(f"wst{i}", [128, D], F32)) for i in range(2)]
        wstB = [E(nc.sbuf_tensor(f"wstB{i}", [128, D], F32)) for i in range(2)]
        wqiB = [E(nc.sbuf_tensor(f"wqiB{i}", [128, D], BF16)) for i in range(2)]
        ivB = [E(nc.sbuf_tensor(f"ivB{i}", [128, 1], F32)) for i in range(2)]
        wub = E(nc.sbuf_tensor("wub", [128, D], F16))
        wub32 = E(nc.sbuf_tensor("wub32", [128, D], F32))
        wib = E(nc.sbuf_tensor("wib", [128, D], I32))
        wqi = [E(nc.sbuf_tensor(f"wqi{i}", [128, D], BF16)) for i in range(2)]
        am_g = E(nc.sbuf_tensor("am_g", [128, IC], F32))
        am_u = E(nc.sbuf_tensor("am_u", [128, IC], F32))
        iv = [E(nc.sbuf_tensor(f"iv{i}", [128, 1], F32)) for i in range(2)]
        wgT = E(nc.sbuf_tensor("wgT", [128, KC, DI], BF16, is_transpose_dest=True))
        wuT = E(nc.sbuf_tensor("wuT", [128, KC, DI], BF16, is_transpose_dest=True))
        sc_g = E(nc.sbuf_tensor("sc_g", [128, IC], F32))
        sc_u = E(nc.sbuf_tensor("sc_u", [128, IC], F32))
        silu_b0 = E(nc.sbuf_tensor("silu_b0", [128, 512], F16))
        silu_b = [silu_b0, silu_b0]
        hp_b0 = E(nc.sbuf_tensor("hp_b0", [128, S], F16))
        hp_b = [hp_b0, hp_b0]
        hmax = E(nc.sbuf_tensor("hmax", [128, 1], F32))
        hm1 = E(nc.sbuf_tensor("hm1", [128, 1], F32))
        mhv = E(nc.sbuf_tensor("mhv", [1, 1], F32))
        roff_t = E(nc.sbuf_tensor("roff_t", [128, 1], F32))
        ps_a = [E(nc.psum_tensor(f"ps_a{i}", [128, 512], F32)) for i in range(2)]
        ps_b = [E(nc.psum_tensor(f"ps_b{i}", [128, 512], F32)) for i in range(2)]
        ps_c = E(nc.psum_tensor("ps_c", [128, 128], F32))

        d_mr = p.dma("sync", mr_s[:, :], ap["mr"])
        d_mx = p.dma("sync", mx_s[:, :], ap["mxall"])
        d_roff = p.do("vector", lambda e: e.memset(roff_t[:, :], MAG), sem=tv)
        d_cl = p.dma("sync", clip_s[:, :], ap["clip_x"])
        d_on = p.dma("sync", ones_s[:, :], ap["ones"])
        d_id = p.dma("sync", id_s[:, :], ap["ident"])

        g1 = p.do("vector", lambda e: e.reduce_max(gm[0:1, :], mx_s[0:1, :], AX.X),
                  deps=[d_mx], sem=tv)
        g2 = p.do("scalar", lambda e: e.activation(sg[0:1, :], clip_s[0:1, :],
                                                   AF.Sigmoid), deps=[d_cl], sem=ta)
        g3 = p.do("vector", lambda e: e.tensor_mul(gm[0:1, :], gm[0:1, :], sg[0:1, :]),
                  deps=[g1, g2], sem=tv)
        g4 = p.do("vector", lambda e: e.tensor_scalar(
            gm[0:1, :], gm[0:1, :], 1.0 / 127.0, None, ALU.mult), deps=[g3], sem=tv)
        g5 = p.do("vector", lambda e: e.reciprocal(sxi[0:1, :], gm[0:1, :]),
                  deps=[g4], sem=tv)
        g6 = p.do("tensor", lambda e: e.matmul(
            ps_c[:, 0:1], ones_s[0:1, :], sxi[0:1, 0:1], start=True, stop=True),
            deps=[g5, d_on], sem=tt)
        g7 = p.do("vector", lambda e: e.tensor_copy(invb[:, :], ps_c[:, 0:1]),
                  deps=[g6], sem=tv)
        g8 = p.do("tensor", lambda e: e.matmul(
            ps_c[:, 0:1], ones_s[0:1, :], gm[0:1, 0:1], start=True, stop=True),
            deps=[g7], sem=tt)
        g9 = p.do("vector", lambda e: e.tensor_copy(sxb[:, :], ps_c[:, 0:1]),
                  deps=[g8], sem=tv)
        a1 = p.do("vector", lambda e: e.tensor_scalar(
            A_t[:, :], mr_s[:, TC:2 * TC], invb[:, 0:1], None, ALU.mult),
            deps=[g7, d_mr], sem=tv)
        a2 = p.do("vector", lambda e: e.tensor_mul(
            B_t[:, :], mr_s[:, 0:TC], A_t[:, :]), deps=[a1], sem=tv)
        a3 = p.do("vector", lambda e: e.tensor_scalar(
            B_t[:, :], B_t[:, :], -1.0, MAG, ALU.mult, ALU.add), deps=[a2], sem=tv)

        ev_x = [[None] * KC for _ in range(TC)]
        xfree = [None, None]
        trf = [None, None]
        txq = [[p.sem(f"txq{i}_{j}") for j in range(4)] for i in range(2)]
        for tc in range(TC):
            b = tc % 2
            deps = [xfree[b]] if xfree[b] else []
            dx = p.dma("sync" if b == 0 else "scalar", x2b[b][:, :],
                       ap["x2"][tc * 128:(tc + 1) * 128, :],
                       deps=deps, sem=(ld if b == 0 else sd))
            q1 = p.do("vector", (lambda e, tc=tc, b=b: e.tensor_scalar(
                ub[0][:, :], x2b[b][:, :], A_t[:, tc:tc + 1], B_t[:, tc:tc + 1],
                ALU.mult, ALU.add)), deps=[dx, a3], sem=tv)
            xfree[b] = q1
            q1b = p.do("vector", (lambda e: e.tensor_scalar(
                ub32[0][:, :], ub[0][:, :], C2, None, ALU.add)),
                deps=[q1], sem=tv)
            q2 = p.do("vector", (lambda e: e.tensor_copy(
                ib_[0][:, :], ub32[0][:, :])), deps=[q1b], sem=tv)
            q3 = p.do("vector", (lambda e: e.tensor_scalar(
                qb_[0][:, :], ib_[0][:, :], 16384.0, 127.0,
                ALU.subtract, ALU.min)), deps=[q2], sem=tv)
            deps = [q3] + (trf[b] if trf[b] else [])
            q4 = p.do("vector", (lambda e, b=b: e.tensor_scalar(
                xq_t[b][:, :], qb_[0][:, :], -128.0, None, ALU.max)),
                deps=deps, sem=tv)
            tl = []
            for c in range(KC):
                tdma = p.dma("sync" if b == 0 else "scalar",
                             xq_fm[:, c, tc * 128:(tc + 1) * 128],
                             xq_t[b][:, c * 128:(c + 1) * 128],
                             deps=[q4], sem=txq[b][c % 4], transpose=True)
                ev_x[tc][c] = tdma
                tl.append(tdma)
            trf[b] = tl[-4:]

        class WPrep:
            def __init__(self, srcn, am_sb, dstT, q, qsem_name, wst_, wqi_, iv_):
                self.src = ap[srcn]
                self.am = am_sb
                self.dstT = dstT
                self.q = q
                self.qsem = p.sem(qsem_name)
                self.tsem = [p.sem(qsem_name + f"_t{j}") for j in range(4)]
                self.wst = wst_
                self.wqi = wqi_
                self.iv = iv_
                self.lfree = [[], []]
                self.q1f = [None, None]
                self.q3f = [[], []]
                self.ev = [[None] * KC for _ in range(IC)]
                self.r_last = None

            def chunk(self, icx):
                b = icx % 2
                dl = p.dma(self.q, self.wst[b][:, :],
                           self.src[icx * 128:(icx + 1) * 128, :],
                           deps=self.lfree[b], sem=self.qsem)
                r1 = p.do("vector", (lambda e, icx=icx, b=b, s=self:
                    e.reduce_max(s.am[:, icx:icx + 1], s.wst[b][:, :], AX.X,
                                 apply_absolute_value=True)),
                    deps=[dl], sem=tv)
                self.r_last = r1
                r2 = p.do("vector", (lambda e, b=b, icx=icx, s=self:
                    e.reciprocal(s.iv[b][:, :], s.am[:, icx:icx + 1])),
                    deps=[r1] + ([self.q1f[b]] if self.q1f[b] else []), sem=tv)
                r3 = p.do("vector", (lambda e, b=b, s=self: e.tensor_scalar(
                    s.iv[b][:, :], s.iv[b][:, :], 127.0, None, ALU.mult)),
                    deps=[r2], sem=tv)
                q1 = p.do("vector", (lambda e, b=b, s=self: e.tensor_scalar(
                    wub[:, :], s.wst[b][:, :], s.iv[b][:, 0:1], MAG,
                    ALU.mult, ALU.add)), deps=[r3], sem=tv)
                self.lfree[b] = [r1, q1]
                self.q1f[b] = q1
                q1b = p.do("vector", (lambda e: e.tensor_scalar(
                    wub32[:, :], wub[:, :], C2, None, ALU.add)),
                    deps=[q1], sem=tv)
                q2 = p.do("vector", (lambda e: e.tensor_copy(
                    wib[:, :], wub32[:, :])), deps=[q1b], sem=tv)
                q3 = p.do("vector", (lambda e, b=b, s=self: e.tensor_scalar(
                    s.wqi[b][:, :], wib[:, :], 16384.0, None, ALU.subtract)),
                    deps=[q2] + (self.q3f[b] if self.q3f[b] else []), sem=tv)
                tl = []
                for c in range(KC):
                    tl.append(p.dma(self.q,
                                    self.dstT[:, c, icx * 128:(icx + 1) * 128],
                                    self.wqi[b][:, c * 128:(c + 1) * 128],
                                    deps=[q3], sem=self.tsem[c % 4],
                                    transpose=True))
                    self.ev[icx][c] = tl[-1]
                self.q3f[b] = tl[-4:]

        wpg = WPrep("wg", am_g, wgT, "sync", "wl_g", wst, wqi, iv)
        wpu = WPrep("wu", am_u, wuT, "scalar", "wl_u", wstB, wqiB, ivB)
        for icx in range(IC):
            wpg.chunk(icx)
            wpu.chunk(icx)
        wg_ev, wg_end = wpg.ev, wpg.r_last
        wu_ev, wu_end = wpu.ev, wpu.r_last

        c1 = p.do("vector", lambda e: e.tensor_scalar(
            sc_g[:, :], am_g[:, :], sxb[:, 0:1], 1.0 / 127.0, ALU.mult, ALU.mult),
            deps=[g9, wg_end], sem=tv)
        c2 = p.do("vector", lambda e: e.tensor_scalar(
            sc_u[:, :], am_u[:, :], sxb[:, 0:1], 1.0 / 127.0, ALU.mult, ALU.mult),
            deps=[g9, wu_end], sem=tv)
        hm0 = p.do("vector", lambda e: e.memset(hmax[:, :], 0.0), sem=tv)

        h_st = []
        psa_free = [None, None]
        psb_free = [None, None]
        silu_free = [None, None]
        hp_free = [[], []]
        hmax_dep = hm0
        for icx in range(IC):
            for tb in range(QB):
                pb2 = (icx * QB + tb) % 2
                mmg = None
                for kc in range(KC):
                    deps = [wg_ev[icx][kc]] + [ev_x[tb * 4 + k][kc]
                                               for k in range(4)]
                    if kc == 0 and psa_free[pb2]:
                        deps.append(psa_free[pb2])
                    mmg = p.do("tensor", (lambda e, icx=icx, tb=tb, kc=kc, pb2=pb2:
                        e.matmul(ps_a[pb2][:, :],
                                 wgT[:, kc, icx * 128:(icx + 1) * 128],
                                 xq_fm[:, kc, tb * 512:(tb + 1) * 512],
                                 start=(kc == 0), stop=(kc == KC - 1))),
                        deps=deps, sem=tt)
                deps = [mmg, c1]
                for sfd in silu_free:
                    if sfd:
                        deps.append(sfd)
                sl = p.do("scalar", (lambda e, icx=icx, pb2=pb2: e.activation(
                    silu_b[pb2][:, :], ps_a[pb2][:, :], AF.Silu,
                    scale=sc_g[:, icx:icx + 1])), deps=deps, sem=ta)
                psa_free[pb2] = sl
                mmu = None
                for kc in range(KC):
                    deps = [wu_ev[icx][kc]] + [ev_x[tb * 4 + k][kc]
                                               for k in range(4)]
                    if kc == 0 and psb_free[pb2]:
                        deps.append(psb_free[pb2])
                    mmu = p.do("tensor", (lambda e, icx=icx, tb=tb, kc=kc, pb2=pb2:
                        e.matmul(ps_b[pb2][:, :],
                                 wuT[:, kc, icx * 128:(icx + 1) * 128],
                                 xq_fm[:, kc, tb * 512:(tb + 1) * 512],
                                 start=(kc == 0), stop=(kc == KC - 1))),
                        deps=deps, sem=tt)
                hb2 = 0
                deps = [mmu, sl, c2] + (hp_free[hb2] if tb == 0 else [])
                hh = p.do("vector", (lambda e, icx=icx, tb=tb, pb2=pb2, hb2=hb2:
                    e.scalar_tensor_tensor(
                        hp_b[hb2][:, tb * 512:(tb + 1) * 512],
                        ps_b[pb2][:, :], sc_u[:, icx:icx + 1],
                        silu_b[pb2][:, :], ALU.mult, ALU.mult)), deps=deps, sem=tv)
                psb_free[pb2] = hh
                silu_free[pb2] = hh
                if tb == QB - 1:
                    sh = p.dma("gpsimd", ap["hd_out"][icx * 128:(icx + 1) * 128, :],
                               hp_b[hb2][:, :], deps=[hh], sem=st)
                    rm = p.do("vector", (lambda e, hb2=hb2: e.reduce_max(
                        hm1[:, :], hp_b[hb2][:, :], AX.X,
                        apply_absolute_value=True)),
                        deps=[hh, hmax_dep], sem=tv)
                    hmax_dep = p.do("vector", (lambda e: e.tensor_max(
                        hmax[:, :], hmax[:, :], hm1[:, :])), deps=[rm], sem=tv)
                    hp_free[hb2] = [sh, rm]
                    h_st.append(sh)
        tpd = p.do("tensor", lambda e: e.transpose(
            ps_c[0:1, 0:128], hmax[:, :], id_s[:, :]),
            deps=[hmax_dep, d_id, g9], sem=tt)
        f1 = p.do("vector", lambda e: e.reduce_max(
            mhv[0:1, :], ps_c[0:1, 0:128], AX.X), deps=[tpd], sem=tv)
        s_mh = p.dma("gpsimd", ap["mh_out"], mhv[0:1, :], deps=[f1], sem=st)
        p.do("gpsimd", lambda e: e.nop(), deps=[s_mh] + h_st[-2:])

        p.build()
    return nc


def build_l3():
    nc = bass.Bass("TRN2", target_bir_lowering=False)
    ap = {}
    for nm, shp, dt in [
        ("hd", [DI, S], F16), ("x2", [S, D], F32), ("mhall", [1, NCORES], F32),
        ("clip_d", [1, 1], F32), ("wd", [D, DI], F32), ("ones", [1, 128], F32),
    ]:
        ap[nm] = nc.dram_tensor(nm, shp, dt, kind="ExternalInput").ap()
    ap["out"] = nc.dram_tensor("out", [S, D], F32, kind="ExternalOutput").ap()
    ap["wdi"] = nc.dram_tensor("wdi", [D, DI], BF16, kind="Internal").ap()
    ap["swd"] = nc.dram_tensor("swd", [1, D], F32, kind="Internal").ap()

    es = contextlib.ExitStack()
    with es:
        E = es.enter_context
        p = Prog(nc, es)
        ld = p.sem("ld"); sd = p.sem("sd"); st = p.sem("st")
        tv = p.sem("tv"); ta = p.sem("ta"); tt = p.sem("tt"); tg = p.sem("tg")

        mh_s = E(nc.sbuf_tensor("mh_s", [1, NCORES], F32))
        clip_s = E(nc.sbuf_tensor("clip_s", [1, 1], F32))
        ones_s = E(nc.sbuf_tensor("ones_s", [1, 128], F32))
        gm = E(nc.sbuf_tensor("gm", [1, 1], F32))
        sg = E(nc.sbuf_tensor("sg", [1, 1], F32))
        shi = E(nc.sbuf_tensor("shi", [1, 1], F32))
        k1 = E(nc.sbuf_tensor("k1", [1, 1], F32))
        invb = E(nc.sbuf_tensor("invb", [128, 1], F32))
        wst = E(nc.sbuf_tensor("wst", [128, DI], F32))
        am_d = E(nc.sbuf_tensor("am_d", [128, KC], F32))
        iv = [E(nc.sbuf_tensor(f"iv{i}", [128, 1], F32)) for i in range(2)]
        wub = E(nc.sbuf_tensor("wub", [128, DI], F16))
        wub32 = E(nc.sbuf_tensor("wub32", [128, DI], F32))
        wib = E(nc.sbuf_tensor("wib", [128, DI], I32))
        wqi = E(nc.sbuf_tensor("wqi", [128, DI], BF16))
        wdT = E(nc.sbuf_tensor("wdT", [128, IC, D], BF16, is_transpose_dest=True))
        srow = E(nc.sbuf_tensor("srow", [1, D], F32))
        srep = E(nc.sbuf_tensor("srep", [128, D], F32))
        hb = [E(nc.sbuf_tensor(f"hb{i}", [128, 1024], F16)) for i in range(2)]
        uq = [E(nc.sbuf_tensor(f"uq{i}", [128, 1024], F16)) for i in range(2)]
        uq32 = [E(nc.sbuf_tensor(f"uq32{i}", [128, 1024], F32)) for i in range(2)]
        iq = [E(nc.sbuf_tensor(f"iq{i}", [128, 1024], I32)) for i in range(2)]
        qq = [E(nc.sbuf_tensor(f"qq{i}", [128, 1024], F32)) for i in range(2)]
        hq = E(nc.sbuf_tensor("hq", [128, IC * 1024], BF16))
        x2b = [E(nc.sbuf_tensor(f"x2b{i}", [128, D], F32)) for i in range(2)]
        t5b = [E(nc.sbuf_tensor(f"t5b{i}", [128, 512], F32)) for i in range(2)]
        roff_t = E(nc.sbuf_tensor("roff_t", [128, 1], F32))
        ob = [E(nc.sbuf_tensor(f"ob{i}", [128, D], F32)) for i in range(2)]
        ps_d = [E(nc.psum_tensor(f"ps_d{i}", [128, 512], F32)) for i in range(2)]
        ps_c = E(nc.psum_tensor("ps_c", [128, 512], F32))

        d_mh = p.dma("sync", mh_s[:, :], ap["mhall"])
        d_cl = p.dma("sync", clip_s[:, :], ap["clip_d"])
        d_roff = p.do("vector", lambda e: e.memset(roff_t[:, :], MAG), sem=tv)
        d_on = p.dma("sync", ones_s[:, :], ap["ones"])

        g1 = p.do("vector", lambda e: e.reduce_max(gm[0:1, :], mh_s[0:1, :], AX.X),
                  deps=[d_mh], sem=tv)
        g2 = p.do("scalar", lambda e: e.activation(sg[0:1, :], clip_s[0:1, :],
                                                   AF.Sigmoid), deps=[d_cl], sem=ta)
        g3 = p.do("vector", lambda e: e.tensor_mul(gm[0:1, :], gm[0:1, :], sg[0:1, :]),
                  deps=[g1, g2], sem=tv)
        g4 = p.do("vector", lambda e: e.tensor_scalar(
            gm[0:1, :], gm[0:1, :], 1.0 / 127.0, None, ALU.mult), deps=[g3], sem=tv)
        g5 = p.do("vector", lambda e: e.reciprocal(shi[0:1, :], gm[0:1, :]),
                  deps=[g4], sem=tv)
        g6 = p.do("vector", lambda e: e.tensor_scalar(
            k1[0:1, :], gm[0:1, :], 1.0 / 127.0, None, ALU.mult), deps=[g4], sem=tv)
        g7 = p.do("tensor", lambda e: e.matmul(
            ps_c[:, 0:1], ones_s[0:1, :], shi[0:1, 0:1], start=True, stop=True),
            deps=[g5, d_on], sem=tt)
        g8 = p.do("vector", lambda e: e.tensor_copy(invb[:, :], ps_c[:, 0:1]),
                  deps=[g7], sem=tv)

        wd_fq = []
        chain = None
        for c in range(KC):
            deps = [chain] if chain else []
            dl = p.dma("gpsimd", wst[:, :], ap["wd"][c * 128:(c + 1) * 128, :],
                       deps=deps)
            r1 = p.do("vector", (lambda e, c=c: e.reduce_max(
                am_d[:, c:c + 1], wst[:, :], AX.X, apply_absolute_value=True)),
                deps=[dl], sem=tv)
            r2 = p.do("vector", (lambda e, c=c: e.reciprocal(
                iv[0][:, :], am_d[:, c:c + 1])), deps=[r1] + ([chain] if chain else []),
                sem=tv)
            r3 = p.do("vector", lambda e: e.tensor_scalar(
                iv[0][:, :], iv[0][:, :], 127.0, None, ALU.mult), deps=[r2], sem=tv)
            q1 = p.do("vector", lambda e: e.tensor_scalar(
                wub[:, :], wst[:, :], iv[0][:, 0:1], MAG, ALU.mult, ALU.add),
                deps=[r3], sem=tv)
            q1b = p.do("vector", lambda e: e.tensor_scalar(
                wub32[:, :], wub[:, :], C2, None, ALU.add), deps=[q1], sem=tv)
            q2 = p.do("vector", lambda e: e.tensor_copy(wib[:, :], wub32[:, :]),
                      deps=[q1b], sem=tv)
            q3 = p.do("vector", lambda e: e.tensor_scalar(
                wqi[:, :], wib[:, :], 16384.0, None, ALU.subtract), deps=[q2], sem=tv)
            sw = p.dma("gpsimd", ap["wdi"][c * 128:(c + 1) * 128, :], wqi[:, :],
                       deps=[q3], sem=st)
            wd_fq.append(sw)
            chain = sw
        wd_T = []
        for icx in range(IC):
            wd_T.append(p.dma("scalar", wdT[:, icx, :],
                              ap["wdi"][:, icx * 128:(icx + 1) * 128],
                              deps=wd_fq, sem=sd, transpose=True))

        sc_st = []
        for c in range(KC):
            sc_st.append(p.dma(
                "gpsimd",
                ap["swd"][0:1, c * 128:(c + 1) * 128].rearrange("one p -> p one"),
                am_d[:, c:c + 1], deps=wd_fq, sem=st))
        d_sr = p.dma("sync", srow[0:1, :], ap["swd"], deps=sc_st)
        z1 = p.do("vector", lambda e: e.tensor_scalar(
            srow[0:1, :], srow[0:1, :], k1[0:1, 0:1], None, ALU.mult),
            deps=[d_sr, g6], sem=tv)
        z2 = p.do("tensor", lambda e: e.matmul(
            ps_c[:, :], ones_s[0:1, :], srow[0:1, 0:512], start=True, stop=True),
            deps=[z1, d_on, g8], sem=tt)
        z3 = p.do("vector", lambda e: e.tensor_copy(srep[:, 0:512], ps_c[:, :]),
                  deps=[z2], sem=tv)
        z4 = p.do("tensor", lambda e: e.matmul(
            ps_c[:, :], ones_s[0:1, :], srow[0:1, 512:1024], start=True, stop=True),
            deps=[z3], sem=tt)
        z5 = p.do("vector", lambda e: e.tensor_copy(srep[:, 512:1024], ps_c[:, :]),
                  deps=[z4], sem=tv)

        out_st = []
        psd_free = [None, None]
        t5_free = [None, None]
        obx_free = [None, None]
        x2_free = [None, None]
        uq_free = [None, None]
        uq32_free = [None, None]
        iq_free = [None, None]
        qq_free = [None, None]
        hb_free = [None, None]
        last_mm = None
        for halfS in range(2):
            hq_done = [None] * IC
            for icx in range(IC):
                b = icx % 2
                deps = [hb_free[b]] if hb_free[b] else []
                dl = p.dma("sync", hb[b][:, :],
                           ap["hd"][icx * 128:(icx + 1) * 128,
                                    halfS * 1024:(halfS + 1) * 1024], deps=deps)
                deps = [dl, g8, d_roff] + ([uq_free[b]] if uq_free[b] else [])
                q1 = p.do("vector", (lambda e, b=b: e.tensor_scalar(
                    uq[b][:, :], hb[b][:, :], invb[:, 0:1], MAG,
                    ALU.mult, ALU.add)), deps=deps, sem=tv)
                hb_free[b] = q1
                deps = [q1] + ([uq32_free[b]] if uq32_free[b] else [])
                q1b = p.do("vector", (lambda e, b=b: e.tensor_scalar(
                    uq32[b][:, :], uq[b][:, :], C2, None, ALU.add)), deps=deps, sem=tv)
                uq_free[b] = q1b
                deps = [q1b] + ([iq_free[b]] if iq_free[b] else [])
                q2 = p.do("vector", (lambda e, b=b: e.tensor_copy(
                    iq[b][:, :], uq32[b][:, :])), deps=deps, sem=tv)
                uq32_free[b] = q2
                deps = [q2] + ([qq_free[b]] if qq_free[b] else [])
                q3 = p.do("vector", (lambda e, b=b: e.tensor_scalar(
                    qq[b][:, :], iq[b][:, :], 16384.0, 127.0, ALU.subtract,
                    ALU.min)), deps=deps, sem=tv)
                iq_free[b] = q3
                deps = [q3] + ([last_mm] if last_mm else [])
                q4 = p.do("vector", (lambda e, icx=icx, b=b: e.tensor_scalar(
                    hq[:, icx * 1024:(icx + 1) * 1024], qq[b][:, :],
                    -128.0, None, ALU.max)), deps=deps, sem=tv)
                qq_free[b] = q4
                hq_done[icx] = q4
            for tch in range(TC // 2):
                tc = halfS * (TC // 2) + tch
                b = tc % 2
                deps = [x2_free[b]] if x2_free[b] else []
                dx = p.dma("scalar", x2b[b][:, :],
                           ap["x2"][tc * 128:(tc + 1) * 128, :], deps=deps, sem=sd)
                odep = []
                for nh2 in range(2):
                    pb2 = nh2
                    mm = None
                    for icx in range(IC):
                        deps = [hq_done[icx], wd_T[icx]]
                        if icx == 0 and psd_free[pb2]:
                            deps.append(psd_free[pb2])
                        mm = p.do("tensor", (lambda e, tch=tch, nh2=nh2, icx=icx,
                                             pb2=pb2: e.matmul(
                            ps_d[pb2][:, :],
                            hq[:, icx * 1024 + tch * 128:icx * 1024 + (tch + 1) * 128],
                            wdT[:, icx, nh2 * 512:(nh2 + 1) * 512],
                            start=(icx == 0), stop=(icx == IC - 1))),
                            deps=deps, sem=tt)
                        last_mm = mm
                    deps = [mm, z3, z5]
                    if t5_free[pb2]:
                        deps.append(t5_free[pb2])
                    e1 = p.do("vector", (lambda e, nh2=nh2, pb2=pb2: e.tensor_mul(
                        t5b[pb2][:, :], ps_d[pb2][:, :],
                        srep[:, nh2 * 512:(nh2 + 1) * 512])), deps=deps, sem=tv)
                    psd_free[pb2] = e1
                    deps = [e1, dx]
                    if obx_free[b]:
                        deps.append(obx_free[b])
                    e2 = p.do("gpsimd", (lambda e, nh2=nh2, pb2=pb2, b=b:
                        e.tensor_add(ob[b][:, nh2 * 512:(nh2 + 1) * 512],
                                     t5b[pb2][:, :],
                                     x2b[b][:, nh2 * 512:(nh2 + 1) * 512])),
                        deps=deps, sem=tg)
                    t5_free[pb2] = e2
                    odep.append(e2)
                so = p.dma("gpsimd", ap["out"][tc * 128:(tc + 1) * 128, :],
                           ob[b][:, :], deps=odep, sem=st)
                obx_free[b] = so
                x2_free[b] = so
                out_st.append(so)
        p.do("gpsimd", lambda e: e.nop(), deps=out_st[-2:])

        p.build()
    return nc


_CACHE = {}


def kernel(L0_attn_qkv, L0_attn_Wq, L0_attn_bq, L0_attn_Wk, L0_attn_bk,
           L0_attn_Wv, L0_attn_bv, L0_attn_Wo, L0_attn_bo,
           L0_ffn_clip_x, L0_ffn_clip_down, L0_ffn_w_gate, L0_ffn_w_up,
           L0_ffn_w_down):
    C128, S128, P_T, tri, ones_row, ident = host_consts()
    f32 = np.float32
    wqT = np.ascontiguousarray(np.asarray(L0_attn_Wq, f32).T)
    wkT = np.ascontiguousarray(np.asarray(L0_attn_Wk, f32).T)
    wvT = np.ascontiguousarray(np.asarray(L0_attn_Wv, f32).T)
    woT = np.ascontiguousarray(np.asarray(L0_attn_Wo, f32).T)
    bq_c = np.ascontiguousarray(np.asarray(L0_attn_bq, f32).reshape(KC, 128).T)
    bk_c = np.ascontiguousarray(np.asarray(L0_attn_bk, f32).reshape(KC, 128).T)
    bv = np.asarray(L0_attn_bv, f32)
    bv_ext = np.zeros((1, NH * 65), f32)
    for h in range(NH):
        bv_ext[0, h * 65:h * 65 + 64] = bv[h * 64:(h + 1) * 64]
    bo_row = np.ascontiguousarray(np.asarray(L0_attn_bo, f32).reshape(1, D))
    xin = np.asarray(L0_attn_qkv, f32)
    wg = np.ascontiguousarray(np.asarray(L0_ffn_w_gate, f32))
    wu = np.ascontiguousarray(np.asarray(L0_ffn_w_up, f32))
    wd = np.ascontiguousarray(np.asarray(L0_ffn_w_down, f32))
    clip_x = np.asarray(L0_ffn_clip_x, f32).reshape(1, 1)
    clip_d = np.asarray(L0_ffn_clip_down, f32).reshape(1, 1)

    if "l1" not in _CACHE:
        _CACHE["l1"] = build_l1()
        _CACHE["l2"] = build_l2()
        _CACHE["l3"] = build_l3()
    nc1, nc2, nc3 = _CACHE["l1"], _CACHE["l2"], _CACHE["l3"]

    core_ids = list(range(NCORES))
    in1 = [{"xin": np.ascontiguousarray(xin[b]), "wqT": wqT, "wkT": wkT,
            "wvT": wvT, "woT": woT, "bq_c": bq_c, "bk_c": bk_c,
            "bv_ext": bv_ext, "bo_row": bo_row, "c128": C128, "s128": S128,
            "p_t": P_T, "tri": tri, "ones": ones_row, "ident": ident}
           for b in range(NCORES)]
    r1 = run_bass_kernel_spmd(nc1, in1, core_ids=core_ids).results

    mxall = np.concatenate([np.asarray(r1[b]["mx_out"]).reshape(1, 1)
                            for b in range(NCORES)], axis=1)
    in2 = [{"x2": r1[b]["x2_out"], "mr": r1[b]["mr_out"], "mxall": mxall,
            "clip_x": clip_x, "wg": wg, "wu": wu, "ones": ones_row,
            "ident": ident} for b in range(NCORES)]
    r2 = run_bass_kernel_spmd(nc2, in2, core_ids=core_ids).results

    mhall = np.concatenate([np.asarray(r2[b]["mh_out"]).reshape(1, 1)
                            for b in range(NCORES)], axis=1)
    in3 = [{"hd": r2[b]["hd_out"], "x2": r1[b]["x2_out"], "mhall": mhall,
            "clip_d": clip_d, "wd": wd, "ones": ones_row}
           for b in range(NCORES)]
    r3 = run_bass_kernel_spmd(nc3, in3, core_ids=core_ids).results

    out = np.stack([np.asarray(r3[b]["out"]) for b in range(NCORES)], axis=0)
    return out.astype(np.float32)
